# version 28
# baseline (speedup 1.0000x reference)
import contextlib
import numpy as np
import ml_dtypes
import concourse.bass as bass
import concourse.mybir as mybir
from concourse.bass_utils import run_bass_kernel_spmd
from concourse.hw_specs import get_activation_tables

F32 = mybir.dt.float32
BF16 = mybir.dt.bfloat16
AF = mybir.ActivationFunctionType
ALU = mybir.AluOpType
AX = mybir.AxisListType

NC = 8
T = 2048
D = 1024
NCH = 8
EPS = 1e-5
INVD = 1.0 / D
NEG = -3.0e38

L0_DILS = [1, 2, 4, 8]
L0_NB = 16
L0_NORM = float(1.0 / np.sqrt(128.0))

_SET_IDX = None
_FUNC_ANY = {AF.Copy, AF.Identity, AF.Relu, AF.Square, AF.Sign, AF.Abs}
_FUNC_SET = {
    AF.Exp: "natural_log_exp_and_others",
    AF.Ln: "natural_log_exp_and_others",
    AF.Sigmoid: "sigmoid_and_others",
    AF.Gelu: "gelu_and_others",
    AF.Silu: "silu_and_others",
}


def _act_set_idx(nc, name):
    global _SET_IDX
    if _SET_IDX is None:
        _SET_IDX = {n: i for i, n in
                    enumerate(get_activation_tables(nc.m.arch).keys())}
    return _SET_IDX[name]


def _f32(x):
    return np.ascontiguousarray(np.asarray(x), dtype=np.float32)


def _pc(v, nch):
    return np.ascontiguousarray(_f32(v).reshape(nch, 128).T)


def host_weights(i):
    w = {}
    w["l0_wqkv"] = np.ascontiguousarray(np.stack(
        [_f32(i["L0_attn_Wk"]), _f32(i["L0_attn_Wq"]), _f32(i["L0_attn_Wv"])],
        axis=0).transpose(0, 1, 3, 2))
    w["l0_wout"] = np.ascontiguousarray(_f32(i["L0_attn_W_out"]).T)
    w["l0_bout"] = _pc(i["L0_attn_b_out"], 8)
    w["l0f_w"] = np.ascontiguousarray(np.stack(
        [_f32(i["L0_ffn_W_r"]).T, _f32(i["L0_ffn_W_k"]).T,
         _f32(i["L0_ffn_W_v"]).T], axis=0))
    w["l0f_mur"] = _pc(i["L0_ffn_mu_r"], 8)
    w["l0f_muk"] = _pc(i["L0_ffn_mu_k"], 8)
    w["l1_wqkv"] = _f32(i["L1_attn_W_qkv"])
    w["l1_wout"] = _f32(i["L1_attn_W_out"])
    w["l1_w1"] = np.ascontiguousarray(_f32(i["L1_ffn_w1_w"]).T)
    w["l1_w2"] = np.ascontiguousarray(_f32(i["L1_ffn_w2_w"]).T)
    w["l1_h1"] = np.ascontiguousarray(_f32(i["L1_ffn_h1_w"]).T)
    w["l1_h2"] = np.ascontiguousarray(_f32(i["L1_ffn_h2_w"]).T)
    w["l1_w3"] = np.ascontiguousarray(_f32(i["L1_ffn_w3_w"]).T)
    w["l1_s1"] = _pc(i["L1_ffn_w1_s"], 32)
    w["l1_s2"] = _pc(i["L1_ffn_w2_s"], 32)
    w["l1_sh1"] = _pc(i["L1_ffn_h1_s"], 32)
    w["l1_sh2"] = _pc(i["L1_ffn_h2_s"], 32)
    w["l1_s3"] = _pc(i["L1_ffn_w3_s"], 8)
    return w


def host_consts():
    c = {}
    c["ident32"] = np.eye(128, dtype=np.float32)
    c["identbf"] = np.eye(128, dtype=ml_dtypes.bfloat16)
    c["onesbf"] = np.ones((128, 128), dtype=ml_dtypes.bfloat16)
    c["zerobf"] = np.zeros((128, 128), dtype=ml_dtypes.bfloat16)
    dh = 64
    inv = 1.0 / (10000.0 ** (np.arange(0, dh, 2, dtype=np.float64) / dh))
    pos = np.arange(256, dtype=np.float64)
    fr = pos[:, None] * inv[None, :]
    fr = np.concatenate([fr, fr], axis=1)
    sc = 1.0 / np.sqrt(dh)
    dup = lambda t: np.ascontiguousarray(
        np.concatenate([t, t], axis=0), dtype=ml_dtypes.bfloat16)
    c["ropeQc"] = dup(np.cos(fr[128:]).T * sc)
    c["ropeQs"] = dup(np.sin(fr[128:]).T * sc)
    c["ropeKc"] = dup(np.cos(fr[128:]).T)
    c["ropeKs"] = dup(np.sin(fr[128:]).T)
    c["ropePc"] = dup(np.cos(fr[:128]).T)
    c["ropePs"] = dup(np.sin(fr[:128]).T)
    P = np.zeros((128, 128), dtype=np.float32)
    for hb in range(2):
        b = hb * 64
        for j in range(32):
            P[b + j, b + 32 + j] = -1.0
            P[b + 32 + j, b + j] = 1.0
    c["pmat"] = np.ascontiguousarray(P.T.astype(ml_dtypes.bfloat16))
    ii = np.arange(128)[:, None]
    jj = np.arange(128)[None, :]
    cm = np.full((128, 256), NEG, dtype=np.float32)
    cm[:, 0:128][jj >= ii] = 0.0
    cm[:, 128:256][jj <= ii] = 0.0
    c["cm2"] = np.ascontiguousarray(
        np.concatenate([cm, cm], axis=1).astype(ml_dtypes.bfloat16))
    cw = np.full((128, 256), NEG, dtype=np.float32)
    cw[:, 128:256][jj <= ii] = 0.0
    c["cmw0"] = np.ascontiguousarray(
        np.concatenate([cw, cw], axis=1).astype(ml_dtypes.bfloat16))
    return c


ENGINES = ("sync", "gpsimd", "tensor", "vector", "scalar")


class Sem:
    __slots__ = ("name", "n", "h")

    def __init__(self, name):
        self.name = name
        self.n = 0
        self.h = None


class Em:
    def __init__(self):
        self.items = []
        self.sems = []
        self.cur_set = None
        self.auto_sems = {}
        self.barrier_exclude = ()

    def sem(self, name):
        s = Sem(name)
        self.sems.append(s)
        return s

    def op(self, eng, fn=None, wait=(), inc=(), dma=False):
        w = [(x, x.n) if isinstance(x, Sem) else x for x in wait]
        if fn is not None and not dma and eng in self.auto_sems:
            a = self.auto_sems[eng]
            w.append((a, a.n))
        ii = []
        for x in inc:
            k = 16 if dma else 1
            if isinstance(x, tuple):
                x, k = x
            x.n += k
            ii.append((x, k))
        self.items.append((eng, fn, w, ii))

    def barrier(self, exclude=(), engines=ENGINES):
        waits = [(s, s.n) for s in self.sems if s.n > 0 and s not in exclude]
        for eng in engines:
            self.items.append((eng, None, list(waits), []))


class Arena:
    def __init__(self, ap8, nbytes):
        self.ap8 = ap8
        self.nbytes = nbytes
        self.off = 0
        self.hwm = 0

    def reset(self):
        self.off = 0

    def view(self, off, shape, dtype):
        esz = 2 if dtype == BF16 else 4
        n = int(np.prod(shape[1:])) * esz
        assert off + n <= self.nbytes
        v = self.ap8[:, off:off + n].bitcast(dtype)
        if len(shape) == 2:
            return v
        dims = "abcdef"[:len(shape) - 1]
        pat = "p (" + " ".join(dims) + ") -> p " + " ".join(dims)
        kw = {d: shape[1 + i] for i, d in enumerate(dims)}
        return v.rearrange(pat, **kw)

    def mark(self):
        return (self.off + 63) & ~63

    def take(self, shape, dtype):
        esz = 2 if dtype == BF16 else 4
        n = int(np.prod(shape[1:])) * esz
        off = (self.off + 63) & ~63
        assert off + n <= self.nbytes, f"arena overflow {off}+{n}>{self.nbytes}"
        self.off = off + n
        self.hwm = max(self.hwm, self.off)
        return self.view(off, shape, dtype)


def build_program(phases):
    first, last = phases[0], phases[-1]
    in_tok = first == "l0a"
    out_tok = last == "l1f"

    nc = bass.Bass("TRN2", target_bir_lowering=False)
    em = Em()
    names = []

    def din(name, shape, dtype=F32):
        names.append(name)
        return nc.dram_tensor(name, list(shape), dtype,
                              kind="ExternalInput").ap()

    import os
    dbg_cat = os.environ.get("K_DBG_CAT") == "1" and phases == ["l0a"]
    dbg_pre = os.environ.get("K_DBG_PRE") == "1" and phases == ["l0a"]
    dbg_att = os.environ.get("K_DBG_ATT") == "1" and phases == ["l0a"]
    l1a_lvl = int(os.environ.get("K_L1A_LVL", "3"))
    xin = din("xin" if in_tok else "xin_fm", (T, D) if in_tok else (D, T))
    dbg_out = (nc.dram_tensor("dbg_cat", [512, T], BF16,
                              kind="ExternalOutput").ap() if dbg_cat else None)
    dbg_pre_out = (nc.dram_tensor("dbg_pre", [D, T], F32,
                                  kind="ExternalOutput").ap() if dbg_pre
                   else None)
    dbg_att_out = (nc.dram_tensor("dbg_att", [4, 128, 16, 128], F32,
                                  kind="ExternalOutput").ap() if dbg_att
                   else None)
    dbg_smd_out = (nc.dram_tensor("dbg_smd", [4, 128, 256], F32,
                                  kind="ExternalOutput").ap() if dbg_att
                   else None)
    dbg_sm2_out = (nc.dram_tensor("dbg_sm2", [4, 128, 576], F32,
                                  kind="ExternalOutput").ap() if dbg_att
                   else None)
    dbg_kqv_out = (nc.dram_tensor("dbg_kqv", [4, 128, 16, 384], F32,
                                  kind="ExternalOutput").ap() if dbg_att
                   else None)
    xout = nc.dram_tensor("xout" if out_tok else "xout_fm",
                          [T, D] if out_tok else [D, T], F32,
                          kind="ExternalOutput").ap()

    cshapes = dict(ident32=((128, 128), F32), identbf=((128, 128), BF16),
                   onesbf=((128, 128), BF16), zerobf=((128, 128), BF16),
                   ropeQc=((128, 128), BF16), ropeQs=((128, 128), BF16),
                   ropeKc=((128, 128), BF16), ropeKs=((128, 128), BF16),
                   ropePc=((128, 128), BF16), ropePs=((128, 128), BF16),
                   pmat=((128, 128), BF16), cm2=((128, 512), BF16),
                   cmw0=((128, 512), BF16))
    need_c = {"ident32", "identbf", "onesbf"}
    if "l0a" in phases:
        need_c |= {"zerobf"}
    if "l1a" in phases:
        need_c |= {"ropeQc", "ropeQs", "ropeKc", "ropeKs", "ropePc", "ropePs",
                   "pmat", "cm2", "cmw0"}
    wshapes = dict(l0_wqkv=(3, 4, 1024, 128), l0_wout=(512, 1024),
                   l0_bout=(128, 8), l0f_w=(3, 1024, 1024),
                   l0f_mur=(128, 8), l0f_muk=(128, 8),
                   l1_wqkv=(1024, 3072), l1_wout=(1024, 1024),
                   l1_w1=(1024, 4096), l1_w2=(1024, 4096),
                   l1_h1=(4096, 4096), l1_h2=(4096, 4096), l1_w3=(4096, 1024),
                   l1_s1=(128, 32), l1_s2=(128, 32), l1_sh1=(128, 32),
                   l1_sh2=(128, 32), l1_s3=(128, 8))
    need_w = set()
    if "l0a" in phases:
        need_w |= {"l0_wqkv", "l0_wout", "l0_bout"}
    if "l0f" in phases:
        need_w |= {"l0f_w", "l0f_mur", "l0f_muk"}
    if "l1a" in phases:
        need_w |= {"l1_wqkv", "l1_wout"}
    if "l1f" in phases:
        need_w |= {"l1_w1", "l1_w2", "l1_h1", "l1_h2", "l1_w3",
                   "l1_s1", "l1_s2", "l1_sh1", "l1_sh2", "l1_s3"}
    cst = {k: din(k, *cshapes[k]) for k in sorted(need_c)}
    wdr = {k: din(k, wshapes[k]) for k in sorted(need_w)}
    wbf = {}
    if "l1f" in phases:
        for k in ("l1_w1", "l1_w2", "l1_h1", "l1_h2", "l1_w3"):
            wbf[k] = nc.dram_tensor("bfc_" + k, list(wshapes[k]), BF16,
                                    kind="Internal").ap()
    if "l1a" in phases:
        wbf["l1_wqkv"] = nc.dram_tensor("bfc_l1_wqkv",
                                        list(wshapes["l1_wqkv"]), BF16,
                                        kind="Internal").ap()

    with contextlib.ExitStack() as ctx:
        E = ctx.enter_context
        x32 = E(nc.sbuf_tensor("x32", [128, NCH, T], F32))
        xbf = E(nc.sbuf_tensor("xbf", [128, NCH, T], BF16))
        csb = {k: E(nc.sbuf_tensor("c_" + k, list(cshapes[k][0]),
                                   cshapes[k][1])) for k in sorted(need_c)}
        smalls = {k: E(nc.sbuf_tensor("w_" + k, list(wshapes[k]), F32))
                  for k in sorted(need_w) if wshapes[k][0] == 128}
        ln_m = E(nc.sbuf_tensor("ln_m", [128, 512], F32))
        epsb = E(nc.sbuf_tensor("epsb", [128, 1], F32))
        junk = E(nc.sbuf_tensor("junk", [128, 8], F32))
        ln_r = E(nc.sbuf_tensor("ln_r", [128, 512], F32))
        ln_t = E(nc.sbuf_tensor("ln_t", [128, 512], F32))
        AB = 98 * 1024
        ar8 = E(nc.sbuf_tensor("arena", [128, AB], mybir.dt.uint8))
        ar = Arena(ar8[:], AB)
        psF6 = E(nc.psum_tensor("psF6", [128, 6, 512], F32))
        psB2 = E(nc.psum_tensor("psB2", [128, 2, 1024], BF16))
        psF = [psF6[:, i, :] for i in range(6)]
        psB = [psB2[:, i, :] for i in range(2)]

        s_ld = em.sem("s_ld")
        s_ti = [em.sem(f"s_ti{k}") for k in range(4)]
        s_wl = em.sem("s_wl")
        s_t = em.sem("s_t")
        s_v = em.sem("s_v")
        s_s = em.sem("s_s")
        s_g = em.sem("s_g")
        s_st = em.sem("s_st")
        s_wa = em.sem("s_wa")
        s_wb = em.sem("s_wb")
        s_wc = em.sem("s_wc")
        em.auto_sems = {"vector": s_v, "scalar": s_s, "gpsimd": s_g}
        em.barrier_exclude = (s_wc,)

        def SYW2(fn, wait=(), inc=(s_wa,)):
            em.op("sync", fn, wait, inc, dma=True)

        def SCD(fn, wait=(), inc=(s_wb,)):
            em.op("scalar", fn, wait, inc, dma=True)

        def SY(fn, wait=(), inc=(s_ld,)):
            em.op("sync", fn, wait, inc, dma=True)

        def GD(fn, wait=(), inc=(s_wl,)):
            em.op("gpsimd", fn, wait, inc, dma=True)

        def mm(out, lhsT, rhs, start, stop, wait=(), inc=()):
            em.op("tensor", lambda e: e.matmul(out, lhsT, rhs, start=start,
                                               stop=stop), wait, inc)

        def tr(out, in_, ident, wait=(), inc=()):
            em.op("tensor", lambda e: e.transpose(out, in_, ident), wait, inc)

        def twait(wait):
            em.op("tensor", None, wait, ())

        def vtt(out, in0, in1, alu, wait=(), inc=(s_v,), eng="vector"):
            em.op(eng, lambda e: e.tensor_tensor(out=out, in0=in0, in1=in1,
                                                 op=alu), wait, inc)

        def vts(out, in0, s1, alu, wait=(), inc=(s_v,), eng="vector"):
            em.op(eng, lambda e: e.tensor_single_scalar(out=out, in_=in0,
                                                        scalar=s1, op=alu),
                  wait, inc)

        def vstt(out, in0, scal, in1, op0, op1, wait=(), inc=(s_v,)):
            em.op("vector", lambda e: e.scalar_tensor_tensor(
                out=out, in0=in0, scalar=scal, in1=in1, op0=op0, op1=op1),
                wait, inc)

        def vcp(out, in_, wait=(), inc=(s_v,), eng="vector"):
            em.op(eng, lambda e: e.tensor_copy(out=out, in_=in_), wait, inc)

        def vred(out, in_, alu, wait=(), inc=(s_v,)):
            em.op("vector", lambda e: e.tensor_reduce(out=out, in_=in_,
                                                      axis=AX.X, op=alu),
                  wait, inc)

        def vrecip(out, in_, wait=(), inc=(s_v,)):
            em.op("vector", lambda e: e.reciprocal(out=out, in_=in_),
                  wait, inc)

        def vmset(ap, val, wait=(), inc=(s_v,), eng="vector"):
            em.op(eng, lambda e: e.memset(ap, val), wait, inc)

        def vsettle(eng="vector"):
            sems = {"vector": s_v, "scalar": s_s, "gpsimd": s_g}
            if eng == "scalar":
                em.op(eng, lambda e: e.memzero(junk[:, 0:1]), (),
                      (sems[eng],))
            else:
                em.op(eng, lambda e: e.memset(junk[:, 0:1], 0.0), (),
                      (sems[eng],))

        def act(out, in_, func, bias=0.0, scale=1.0, accum_out=None,
                wait=(), inc=(s_s,)):
            if func not in _FUNC_ANY:
                sname = _FUNC_SET[func]
                if em.cur_set != sname:
                    sid = _act_set_idx(nc, sname)
                    em.op("scalar", lambda e, sid=sid: e.add_instruction(
                        mybir.InstLoadActFuncSet(
                            name=e.bass.get_next_instruction_name(),
                            act_func_set_id=sid, ins=[], outs=[])))
                    em.cur_set = sname
            em.op("scalar", lambda e: e.activation(
                out=out, in_=in_, func=func, bias=bias, scale=scale,
                accum_out=accum_out), wait, inc)

        for k in sorted(need_c):
            SY(lambda e, d=csb[k], s=cst[k]: e.dma_start(out=d[:], in_=s))
        for k in sorted(smalls):
            SY(lambda e, d=smalls[k], s=wdr[k]: e.dma_start(out=d[:], in_=s))

        vmset_done = em.op("vector", lambda e: e.memset(epsb[:], EPS), (), ((s_v, 1),))
        if in_tok:
            tin = ar.view(34 * 1024, (128, 16, 1024), F32)
            xin_r = xin.rearrange("(tb p) c -> p tb c", p=128)
            for k in range(4):
                SY(lambda e, d=tin[:, 4 * k:4 * k + 4, :],
                   s=xin_r[:, 4 * k:4 * k + 4, :]:
                   e.dma_start(out=d, in_=s), inc=(s_ti[k],))
            rec_in = {0: None, 1: None}
            for tb in range(16):
                for cg in range(2):
                    bank = psF[cg % 2]
                    for j in range(4):
                        w0 = ()
                        if j == 0:
                            w0 = [(s_ti[tb // 4], 16)]
                            if tb == 0 and cg == 0:
                                w0 = w0 + [(s_ld, s_ld.n)]
                            if rec_in[cg % 2]:
                                w0 = w0 + [rec_in[cg % 2]]
                        tr(bank[:, j * 128:(j + 1) * 128],
                           tin[:, tb, (cg * 4 + j) * 128:
                               (cg * 4 + j) * 128 + 128],
                           csb["ident32"][:],
                           wait=w0,
                           inc=(s_t,) if j == 3 else ())
                    vcp(x32[:, cg * 4:cg * 4 + 4, tb * 128:(tb + 1) * 128],
                        bank[:].rearrange("p (a b) -> p a b", a=4),
                        wait=[(s_t, s_t.n)])
                    vcp(xbf[:, cg * 4:cg * 4 + 4, tb * 128:(tb + 1) * 128],
                        bank[:].rearrange("p (a b) -> p a b", a=4))
                    rec_in[cg % 2] = (s_v, s_v.n)
            ar.reset()
        else:
            SY(lambda e: e.dma_start(
                out=x32[:], in_=xin.rearrange("(c p) t -> p c t", p=128)))
            GD(lambda e: e.dma_start(
                out=xbf[:], in_=xin.rearrange("(c p) t -> p c t", p=128)))
        wc_gate = {}

        def emit_cache(keys):
            keys = [k for k in keys if k in wbf]
            for k in keys:
                kk = wdr[k].shape[0]
                for half in range(2 if kk >= 4096 else 1):
                    hh = kk // 2 if kk >= 4096 else kk
                    GD(lambda e, d=wbf[k][half * hh:(half + 1) * hh, :],
                        src=wdr[k][half * hh:(half + 1) * hh, :]:
                        e.dma_start(out=d, in_=src), inc=(s_wc,))
                wc_gate[k] = s_wc.n
        em.barrier(exclude=em.barrier_exclude)

        def emit_ln(sq_scr, outbf):
            base = [(s_v, s_v.n), (s_s, s_s.n)]
            for tt in range(4):
                emit_ln_tile(tt, sq_scr, outbf, base=base)

        ln_rec = {"m": (s_v, 0), "v": (s_v, 0), "ap": (s_v, 0),
                  "sq": (s_t, 0)}

        def emit_ln_tile(tt, sq_scr, outbf, base=None):
            tc = slice(tt * 512, tt * 512 + 512)
            ybf = xbf[:, :, tc]
            act(sq_scr, x32[:, :, tc], AF.Square,
                wait=([base[0], ln_rec["sq"]] if base
                      else [(s_v, s_v.n)]), inc=())
            vsettle("scalar")
            vcp(ybf, x32[:, :, tc], inc=())
            vsettle()
            twait([(s_v, s_v.n), (s_s, s_s.n)])
            for c in range(NCH):
                mm(psF[4][:], csb["onesbf"][:], ybf[:, c, :],
                   start=(c == 0), stop=(c == 7),
                   wait=[ln_rec["m"]] if c == 0 else (),
                   inc=(s_t,) if c == 7 else ())
            for c in range(NCH):
                mm(psF[5][:], csb["onesbf"][:], sq_scr[:, c, :],
                   start=(c == 0), stop=(c == 7),
                   wait=[ln_rec["v"]] if c == 0 else (),
                   inc=(s_t,) if c == 7 else ())
            ln_rec["sq"] = (s_t, s_t.n)
            vts(ln_m[:], psF[4][:], INVD, ALU.mult,
                wait=[(s_t, s_t.n), ln_rec["ap"]])
            ln_rec["m"] = (s_v, s_v.n)
            vtt(ln_t[:], ln_m[:], ln_m[:], ALU.mult)
            vstt(ln_t[:], psF[5][:], INVD, ln_t[:], ALU.mult, ALU.subtract)
            ln_rec["v"] = (s_v, s_v.n)
            act(ln_t[:], ln_t[:], AF.Ln, bias=epsb[:], wait=[(s_v, s_v.n)])
            act(ln_r[:], ln_t[:], AF.Exp, scale=-0.5, inc=(s_s,))
            mb = ln_m[:].rearrange("p (a b) -> p a b", a=1
                                   ).broadcast_to((128, NCH, 512))
            rb = ln_r[:].rearrange("p (a b) -> p a b", a=1
                                   ).broadcast_to((128, NCH, 512))
            x3d = x32[:, :, tc]
            vtt(x3d, x3d, mb, ALU.subtract, wait=[(s_s, s_s.n)])
            vtt(x3d, x3d, rb, ALU.mult)
            ln_rec["ap"] = (s_v, s_v.n)
            if outbf:
                vcp(xbf[:, :, tc], x3d)

        def emit_l0a():
            ar.reset()
            wqkv = ar.take((128, 3, 4, 8, 128), BF16)
            wout = ar.take((128, 4, 1024), BF16)
            kqv_mark = ar.mark()
            kqv = ar.take((128, 16, 384), BF16)
            tmpd = ar.take((128, 16, 128), BF16)
            kqd = ar.take((128, 16, 16), F32)
            smdbf = ar.take((128, 16, 16), BF16)
            mx = ar.take((128, 16), F32)
            nb = ar.take((128, 16), F32)
            ssum = ar.take((128, 16), F32)
            rr = ar.take((128, 16), F32)
            att = ar.take((128, 16, 128), F32)
            attbf = ar.take((128, 16, 128), BF16)
            cat = ar.take((128, 4, 2048), BF16)
            sq_scr = ar.view(kqv_mark, (128, 8, 512), BF16)
            GD(lambda e: e.dma_start(
                out=wqkv, in_=wdr["l0_wqkv"].rearrange(
                    "a b (c p) k -> p a b c k", p=128)))
            GD(lambda e: e.dma_start(
                out=wout, in_=wdr["l0_wout"].rearrange("(v p) o -> p v o",
                                                       p=128)))
            vmset(cat, 0.0)
            emit_cache(("l1_wqkv", "l1_w1", "l1_w2"))
            em.barrier(exclude=em.barrier_exclude)

            evac_hist = []
            rec_scatter = (s_v, 0)
            rec_exp = (s_s, 0)
            for h, dil in enumerate(L0_DILS):
                L = L0_NB // dil
                ngroups = (L + 3) // 4
                for g in range(ngroups):
                    gs = min(4, L - g * 4)
                    bks = [psF[(g % 2) * 3 + j] for j in range(3)]
                    for j in range(gs):
                        n = g * 4 + j
                        Bn = n * dil
                        for c in range(NCH):
                            for proj in range(3):
                                w0 = ()
                                if (c == 0 and j == 0 and proj == 0 and
                                        len(evac_hist) >= 2):
                                    w0 = [(s_v, evac_hist[-2])]
                                mm(bks[proj][:, j * 128:(j + 1) * 128],
                                   xbf[:, c, Bn * 128:(Bn + 1) * 128],
                                   wqkv[:, proj, h, c, :],
                                   start=(c == 0), stop=(c == 7),
                                   wait=w0,
                                   inc=(s_t,) if (c == 7 and proj == 2 and
                                                  j == gs - 1) else ())
                    wt_ = [(s_t, s_t.n)]
                    for proj in range(3):
                        vcp(kqv[:, g * 4:g * 4 + gs,
                                proj * 128:(proj + 1) * 128],
                            bks[proj][:, 0:gs * 128].rearrange(
                                "p (a b) -> p a b", a=gs),
                            wait=wt_ if proj == 0 else ())
                    evac_hist.append(s_v.n)
                vmset(kqd[:, 0:L, :], NEG, wait=[rec_exp])
                for n in range(L):
                    vtt(tmpd[:, 0:n + 1, :],
                        kqv[:, n:n + 1, 0:128].broadcast_to((128, n + 1, 128)),
                        kqv[:, 0:n + 1, 128:256], ALU.mult)
                    vred(kqd[:, n, 0:n + 1], tmpd[:, 0:n + 1, :], ALU.add)
                k3 = kqd[:, 0:L, :]
                vred(mx[:, 0:L], k3, ALU.max)
                vts(nb[:, 0:L], mx[:, 0:L], -1.0, ALU.mult)
                vtt(k3, k3, nb[:, 0:L].rearrange("p (a b) -> p a b", b=1
                                                 ).broadcast_to((128, L, 16)),
                    ALU.add)
                act(k3, k3, AF.Exp, scale=L0_NORM, wait=[(s_v, s_v.n)])
                rec_exp = (s_s, s_s.n)
                vred(ssum[:, 0:L], k3, ALU.add, wait=[rec_exp])
                vrecip(rr[:, 0:L], ssum[:, 0:L])
                vtt(smdbf[:, 0:L, :], k3,
                    rr[:, 0:L].rearrange("p (a b) -> p a b", b=1
                                         ).broadcast_to((128, L, 16)),
                    ALU.mult)
                vmset(att[:, 0:L, :], 0.0)
                for n in range(L):
                    vtt(tmpd[:, 0:n + 1, :],
                        kqv[:, n:n + 1, 256:384].broadcast_to(
                            (128, n + 1, 128)),
                        smdbf[:, n, 0:n + 1].rearrange(
                            "p (a b) -> p a b", b=1).broadcast_to(
                            (128, n + 1, 128)), ALU.mult)
                    vtt(att[:, 0:n + 1, :], att[:, 0:n + 1, :],
                        tmpd[:, 0:n + 1, :], ALU.add)
                vcp(attbf[:, 0:L, :], att[:, 0:L, :], inc=())
                vsettle()
                twait([(s_v, s_v.n)])
                for m in range(L):
                    tr(psB[(m // 8) % 2][:, (m % 8) * 128:(m % 8) * 128 + 128],
                       attbf[:, m, :], csb["identbf"][:],
                       wait=[rec_scatter] if m == 0 else (),
                       inc=(s_t,) if (m % 8 == 7 or m == L - 1) else ())
                wt_ = [(s_t, s_t.n)]
                catv = cat[:, h, :].rearrange("p (s x) -> p s x", x=16)
                for m in range(L):
                    vtt(catv[:, :, m * dil],
                        psB[(m // 8) % 2][:, (m % 8) * 128:(m % 8) * 128 + 128],
                        csb["zerobf"][:], ALU.add, wait=wt_ if m == 0 else ())
                rec_scatter = (s_v, s_v.n)
            em.barrier(exclude=em.barrier_exclude)
            rec_op = {0: None, 1: None}
            first_op = True
            for tt in range(4):
                tc = slice(tt * 512, tt * 512 + 512)
                for oc in range(NCH):
                    bk = psF[oc % 2]
                    for cv in range(4):
                        w0 = ()
                        if cv == 0:
                            w0 = ([(s_v, s_v.n)] if first_op else
                                  ([rec_op[oc % 2]] if rec_op[oc % 2]
                                   else []))
                            first_op = False
                        mm(bk[:], wout[:, cv, oc * 128:(oc + 1) * 128],
                           cat[:, cv, tc], start=(cv == 0), stop=(cv == 3),
                           wait=w0, inc=(s_t,) if cv == 3 else ())
                    vstt(x32[:, oc, tc], bk[:], smalls["l0_bout"][:, oc:oc + 1],
                         x32[:, oc, tc], ALU.add, ALU.add,
                         wait=[(s_t, s_t.n)])
                    rec_op[oc % 2] = (s_v, s_v.n)
            em.barrier(exclude=em.barrier_exclude)
            if dbg_cat:
                SY(lambda e: e.dma_start(
                    out=dbg_out.rearrange("(v p) t -> p v t", p=128),
                    in_=cat[:]),
                    wait=[(s_v, s_v.n), (s_s, s_s.n), (s_t, s_t.n)],
                    inc=(s_st,))
                em.barrier(exclude=em.barrier_exclude)
            if dbg_pre:
                SY(lambda e: e.dma_start(
                    out=dbg_pre_out.rearrange("(c p) t -> p c t", p=128),
                    in_=x32[:]),
                    wait=[(s_v, s_v.n), (s_s, s_s.n), (s_t, s_t.n)],
                    inc=(s_st,))
                em.barrier(exclude=em.barrier_exclude)
            emit_ln(sq_scr, outbf=False)

        def emit_l0f():
            ar.reset()
            wf = ar.take((128, 3, 8, 1024), BF16)
            d_mark = ar.mark()
            d32 = ar.take((128, 8, 512), BF16)
            xr2 = [ar.take((128, 8, 512), BF16) for _ in range(2)]
            xk2 = [ar.take((128, 8, 512), BF16) for _ in range(2)]
            rbf = ar.take((128, 8, 512), BF16)
            tprod = ar.take((128, 1, 512), F32)
            sq_scr = ar.view(d_mark, (128, 8, 512), BF16)
            mur = smalls["l0f_mur"]
            muk = smalls["l0f_muk"]
            GD(lambda e: e.dma_start(
                out=wf, in_=wdr["l0f_w"].rearrange("a (c p) o -> p a c o",
                                                   p=128)),
                wait=[(s_v, s_v.n), (s_s, s_s.n), (s_t, s_t.n)])
            emit_cache(("l1_h1",))
            em.barrier(exclude=em.barrier_exclude)
            TQ = 512
            rec_free = {0: (s_t, 0), 1: (s_t, 0)}
            rec_bank = {0: (s_v, 0), 1: (s_v, 0)}
            prep_done = {}

            def prep(q):
                t0 = q * TQ
                xr = xr2[q % 2]
                xk = xk2[q % 2]
                if q == 0:
                    vtt(d32[:, :, 1:TQ], x32[:, :, 1:TQ],
                        x32[:, :, 0:TQ - 1], ALU.subtract,
                        wait=[rec_free[q % 2]])
                    vcp(d32[:, :, 0:1], x32[:, :, 0:1])
                else:
                    vtt(d32[:], x32[:, :, t0:t0 + TQ],
                        x32[:, :, t0 - 1:t0 + TQ - 1], ALU.subtract,
                        wait=[rec_free[q % 2]])
                for c in range(NCH):
                    lo = 1 if q == 0 else 0
                    vstt(xr[:, c, lo:TQ], d32[:, c, lo:TQ],
                         mur[:, c:c + 1],
                         x32[:, c, t0 + lo - 1:t0 + TQ - 1],
                         ALU.mult, ALU.add)
                    vstt(xk[:, c, lo:TQ], d32[:, c, lo:TQ],
                         muk[:, c:c + 1],
                         x32[:, c, t0 + lo - 1:t0 + TQ - 1],
                         ALU.mult, ALU.add)
                if q == 0:
                    mb = mur[:].rearrange("p (a b) -> p a b", b=1)
                    kb = muk[:].rearrange("p (a b) -> p a b", b=1)
                    vtt(xr[:, :, 0:1], d32[:, :, 0:1], mb, ALU.mult)
                    vtt(xk[:, :, 0:1], d32[:, :, 0:1], kb, ALU.mult)
                prep_done[q] = (s_v, s_v.n)

            prep(0)
            for q in range(T // TQ):
                t0 = q * TQ
                tc = slice(t0, t0 + TQ)
                xr = xr2[q % 2]
                xk = xk2[q % 2]
                kact = xr2[q % 2]
                if q + 1 < T // TQ:
                    prep(q + 1)
                for oc in range(NCH):
                    bk = psF[oc % 2]
                    for c in range(NCH):
                        w0 = ()
                        if c == 0:
                            w0 = [prep_done[q], rec_bank[oc % 2]]
                        mm(bk[:], wf[:, 0, c, oc * 128:(oc + 1) * 128],
                           xr[:, c, :], start=(c == 0), stop=(c == 7),
                           wait=w0, inc=(s_t,) if c == 7 else ())
                    act(rbf[:, oc, :], bk[:], AF.Sigmoid,
                        wait=[(s_t, s_t.n)])
                    rec_bank[oc % 2] = (s_s, s_s.n)
                for oc in range(NCH):
                    bk = psF[oc % 2]
                    for c in range(NCH):
                        w0 = ()
                        if c == 0:
                            w0 = [rec_bank[oc % 2]]
                        mm(bk[:], wf[:, 1, c, oc * 128:(oc + 1) * 128],
                           xk[:, c, :], start=(c == 0), stop=(c == 7),
                           wait=w0, inc=(s_t,) if c == 7 else ())
                    vts(ln_t[:], bk[:], 0.0, ALU.max,
                        wait=[(s_t, s_t.n)])
                    vtt(kact[:, oc, :], ln_t[:], ln_t[:], ALU.mult)
                    rec_bank[oc % 2] = (s_v, s_v.n)
                kact_done = (s_v, s_v.n)
                for oc in range(NCH):
                    bk = psF[oc % 2]
                    for c in range(NCH):
                        w0 = ()
                        if c == 0:
                            w0 = [kact_done] if oc == 0 else []
                            w0 = list(w0) + [rec_bank[oc % 2]]
                        mm(bk[:], wf[:, 2, c, oc * 128:(oc + 1) * 128],
                           kact[:, c, :], start=(c == 0), stop=(c == 7),
                           wait=w0, inc=(s_t,) if c == 7 else ())
                    vtt(tprod[:, 0, :], rbf[:, oc, :], bk[:],
                        ALU.mult, wait=[(s_t, s_t.n)])
                    vtt(x32[:, oc, tc], x32[:, oc, tc],
                        tprod[:, 0, :], ALU.add)
                    rec_bank[oc % 2] = (s_v, s_v.n)
                rec_free[q % 2] = (s_t, s_t.n)
            em.barrier(exclude=em.barrier_exclude)
            emit_ln(sq_scr, outbf=True)

        def emit_l1a():
            ar.reset()
            wout = ar.take((128, 8, 1024), BF16)
            wsl = [ar.take((128, 8, 512), BF16) for _ in range(2)]
            qk_mark = ar.mark()
            qk = ar.take((128, 16, 512), BF16)
            vraw_mark = ar.mark()
            vraw = ar.take((128, 8, 512), BF16)
            rkP = ar.take((128, 8, 512), BF16)
            kPprev = ar.take((128, 8, 128), BF16)
            vtm_mark = ar.mark()
            vtm = ar.take((128, 4, 8, 128), BF16)
            vtmprev = ar.take((128, 8, 128), BF16)
            simsb_mark = ar.mark()
            simsb = ar.take((128, 8, 256), F32)
            pexp2 = [ar.view(simsb_mark, (128, 8, 256), BF16),
                     ar.view(simsb_mark + 4096, (128, 8, 256), BF16)]
            aout_mark = ar.mark()
            aout = ar.take((128, 8, 512), BF16)
            pq_all = ar.view(aout_mark, (128, 8, 512), BF16)
            pk_all = ar.view(simsb_mark, (128, 8, 512), BF16)
            rtmp = ar.view(vtm_mark, (128, 2048), F32)
            pbf = ar.view(vraw_mark, (128, 8, 256), BF16)
            attT = ar.view(vraw_mark + 4096, (128, 8, 256), BF16)
            sq_scr = ar.view(qk_mark, (128, 8, 512), BF16)
            mx = ln_t[:, 0:8]
            nb8 = ln_t[:, 8:16]
            ssum = ln_t[:, 16:24]
            rr = ln_t[:, 24:32]
            GD(lambda e: e.dma_start(
                out=wout, in_=wdr["l1_wout"].rearrange("(c p) o -> p c o",
                                                       p=128)),
                wait=[(s_v, s_v.n), (s_s, s_s.n), (s_t, s_t.n)])
            em.barrier(exclude=em.barrier_exclude)
            t13 = rtmp[:].bitcast(BF16).rearrange(
                "p (a b c) -> p a b c", a=8, b=4, c=128)

            def tbl(name):
                return csb[name][:].rearrange("p (a b) -> p a b", a=1
                                              ).broadcast_to((128, 4, 128))

            SLOT8 = (0, 2, 4, 6, 1, 3, 5, 7)
            rec_avbank = [(s_s, 0)] * 2
            rec_pbf = (s_t, 0)
            exp_rec = [(s_s, 0)] * 4
            pbf_done = {-1: (s_v, 0)}
            sim_done = {}
            attT_done = {}
            av_done = {}

            for ss in range(4):
                c0 = ss * 512
                scs = slice(c0, c0 + 512)
                base_wa, base_wb = s_wa.n, s_wb.n

                def qslab(gg, gate=None):
                    w = [(s_t, s_t.n)]
                    if gate is not None:
                        w = w + [(s_wc, gate)]
                    SYW2(lambda e, d=wsl[gg % 2][:, 0:4, :], o0=gg * 512:
                        e.dma_start(out=d, in_=wbf["l1_wqkv"][
                            0:512, o0:o0 + 512].rearrange(
                            "(c p) o -> p c o", p=128)), wait=w)
                    SCD(lambda e, d=wsl[gg % 2][:, 4:8, :], o0=gg * 512:
                        e.dma_start(out=d, in_=wbf["l1_wqkv"][
                            512:1024, o0:o0 + 512].rearrange(
                            "(c p) o -> p c o", p=128)), wait=w)
                qslab(0, wc_gate["l1_wqkv"] if ss == 0 else None)
                rec_qbank = [(s_v, 0), (s_s, 0)]
                for g in range(6):
                    if g + 1 < 6:
                        qslab(g + 1)
                    for j in range(4):
                        oc = g * 4 + j
                        bk = psF[oc % 2]
                        for c in range(NCH):
                            w0 = ()
                            if c == 0:
                                w0 = [(s_wa, base_wa + 16 * (g + 1)),
                                      (s_wb, base_wb + 16 * (g + 1)),
                                      rec_qbank[oc % 2]]
                            mm(bk[:], wsl[g % 2][:, c, j * 128:(j + 1) * 128],
                               xbf[:, c, scs], start=(c == 0), stop=(c == 7),
                               wait=w0, inc=(s_t,) if c == 7 else ())
                        dst = (qk[:, oc, :] if oc < 16 else
                               vraw[:, oc - 16, :])
                        if oc % 2 == 0:
                            vcp(dst, bk[:], wait=[(s_t, s_t.n)])
                            rec_qbank[0] = (s_v, s_v.n)
                        else:
                            act(dst, bk[:], AF.Copy, wait=[(s_t, s_t.n)])
                            rec_qbank[1] = (s_s, s_s.n)
                n_qkv_v, n_qkv_s = s_v.n, s_s.n
                for r in range(4):
                    c0r = 2 * r
                    w0 = ([(s_s, s_s.n)] if r > 0 else
                          [(s_v, n_qkv_v), (s_s, n_qkv_s)])
                    mm(psF[2][:], csb["pmat"][:], qk[:, c0r, :],
                       start=True, stop=True, wait=w0)
                    mm(psF[3][:], csb["pmat"][:], qk[:, c0r + 1, :],
                       start=True, stop=True)
                    mm(psF[4][:], csb["pmat"][:], qk[:, 8 + c0r, :],
                       start=True, stop=True)
                    mm(psF[5][:], csb["pmat"][:], qk[:, 9 + c0r, :],
                       start=True, stop=True, inc=(s_t,))
                    act(pq_all[:, c0r:c0r + 2, :], psF6[:, 2:4, :], AF.Copy,
                        wait=[(s_t, s_t.n)], inc=())
                    act(pk_all[:, c0r:c0r + 2, :], psF6[:, 4:6, :], AF.Copy,
                        inc=(s_s,))
                n_rope_s = s_s.n
                q4 = qk[:, 0:8, :].rearrange("p a (b c) -> p a b c", c=128)
                k4 = qk[:, 8:16, :].rearrange("p a (b c) -> p a b c", c=128)
                rp4 = rkP[:].rearrange("p a (b c) -> p a b c", c=128)
                pq4 = pq_all[:].rearrange("p a (b c) -> p a b c", c=128)
                pk4 = pk_all[:].rearrange("p a (b c) -> p a b c", c=128)

                def tbl4(name):
                    return csb[name][:].rearrange(
                        "p (a b c) -> p a b c", a=1, b=1).broadcast_to(
                        (128, 8, 4, 128))
                wt_ = [(s_s, n_rope_s)]
                vtt(t13, pq4, tbl4("ropeQs"), ALU.mult, wait=wt_)
                vtt(q4, q4, tbl4("ropeQc"), ALU.mult)
                vtt(q4, q4, t13, ALU.add)
                vtt(t13, pk4, tbl4("ropePs"), ALU.mult)
                vtt(rp4, k4, tbl4("ropePc"), ALU.mult)
                vtt(rp4, rp4, t13, ALU.add)
                vtt(t13, pk4, tbl4("ropeKs"), ALU.mult)
                vtt(k4, k4, tbl4("ropeKc"), ALU.mult)
                vtt(k4, k4, t13, ALU.add)
                n_chain = s_v.n
                rec_vt = {0: None, 1: None}
                for wl in range(4):
                    w0 = ()
                    if wl == 0:
                        w0 = [(s_v, n_qkv_v), (s_s, n_qkv_s)]
                    elif rec_vt[wl % 2]:
                        w0 = [rec_vt[wl % 2]]
                    for ch in range(NCH):
                        tr(psB[wl % 2][:, ch * 128:(ch + 1) * 128],
                           vraw[:, ch, wl * 128:(wl + 1) * 128],
                           csb["identbf"][:],
                           wait=w0 if ch == 0 else (),
                           inc=(s_t,) if ch == 7 else ())
                    vcp(vtm[:, wl, :, :],
                        psB[wl % 2][:].rearrange("p (a b) -> p a b", a=8),
                        wait=[(s_t, s_t.n)])
                    rec_vt[wl % 2] = (s_v, s_v.n)
                whgs = [(wl, hg) for wl in range(4) for hg in range(2)]
                sim_done.clear()
                pbf_done.clear()
                pbf_done[-1] = pbf_done[-2] = (s_v, 0)
                attT_done.clear()

                def emit_fill_sims(i):
                    wl, hg = whgs[i]
                    w = ss * 4 + wl
                    wc = slice(wl * 128, (wl + 1) * 128)
                    mk = "cmw0" if w == 0 else "cm2"
                    for b in range(4):
                        wf_ = [exp_rec[b]]
                        if i == 0:
                            wf_.append((s_s, n_rope_s))
                        mm(psF[b][:], csb["identbf"][:], csb[mk][:],
                           start=True, stop=True, wait=wf_)
                    for sl in range(8):
                        hh = SLOT8[sl]
                        h = hg * 8 + hh
                        ch = h // 2
                        base = (h % 2) * 64
                        bk = psF[sl // 2]
                        off = (sl % 2) * 256
                        lhs = qk[base:base + 64, ch, wc]
                        if i == 0 and sl == 0:
                            twait([(s_v, n_chain)])
                        if w > 0:
                            rhsP = (kPprev[base:base + 64, ch, :]
                                    if wl == 0 else
                                    rkP[base:base + 64, ch,
                                        (wl - 1) * 128:wl * 128])
                            mm(bk[:, off:off + 128], lhs, rhsP,
                               start=False, stop=True)
                        mm(bk[:, off + 128:off + 256], lhs,
                           qk[base:base + 64, 8 + ch, wc],
                           start=False, stop=True,
                           inc=(s_t,) if sl == 7 else ())
                    sim_done[i] = (s_t, s_t.n)

                def emit_trans(i):
                    nonlocal rec_pbf
                    wl, hg = whgs[i]
                    w = ss * 4 + wl
                    twait([pbf_done[i]])
                    for jc in range(2):
                        if w == 0 and jc == 0:
                            continue
                        for sl in range(8):
                            tr(psB[jc][:, sl * 128:sl * 128 + 128],
                               pbf[:, sl, jc * 128:(jc + 1) * 128],
                               csb["identbf"][:],
                               inc=(s_t,) if sl == 7 else ())
                    rec_pbf = (s_t, s_t.n)

                def emit_exp(i):
                    pe = pexp2[i % 2]
                    act(pe[:],
                        psF6[:, 0:4, :].rearrange("p a (b c) -> p (a b) c",
                                                  c=256),
                        AF.Exp, wait=[sim_done[i], pbf_done[i - 2]],
                        inc=(s_s,))
                    for b in range(4):
                        exp_rec[b] = (s_s, s_s.n)

                def emit_attTcp(i):
                    wl, hg = whgs[i]
                    w = ss * 4 + wl
                    if w == 0:
                        vcp(attT[:, :, 128:256],
                            psB[1][:].rearrange("p (a b) -> p a b", a=8),
                            wait=[rec_pbf])
                    else:
                        vcp(attT[:].rearrange("p a (b c) -> p a b c", c=128),
                            psB2[:].rearrange("p a (b c) -> p b a c", c=128),
                            wait=[rec_pbf])
                    attT_done[i] = (s_v, s_v.n)

                def emit_sumpbf(i):
                    pe = pexp2[i % 2]
                    vred(ssum[:, 0:8], pe[:], ALU.add,
                         wait=[(s_s, exp_rec[3][1])])
                    vrecip(rr[:, 0:8], ssum[:, 0:8])
                    vtt(pbf[:], pe[:],
                        rr[:, 0:8].rearrange("p (a b) -> p a b", b=1
                                             ).broadcast_to((128, 8, 256)),
                        ALU.mult, wait=[rec_pbf])
                    pbf_done[i] = (s_v, s_v.n)

                def emit_avm(i):
                    wl, hg = whgs[i]
                    w = ss * 4 + wl
                    for sl in range(8):
                        hh = SLOT8[sl]
                        h = hg * 8 + hh
                        ch = h // 2
                        bk = psF[4 + sl // 4]
                        off = (sl % 4) * 128
                        w0 = []
                        if sl == 0:
                            w0 = [attT_done[i], rec_avbank[0]]
                        elif sl == 4:
                            w0 = [rec_avbank[1]]
                        if w > 0:
                            vprev = (vtmprev[:, ch, :] if wl == 0 else
                                     vtm[:, wl - 1, ch, :])
                            mm(bk[:, off:off + 128], vprev,
                               attT[:, sl, 0:128], start=True,
                               stop=False, wait=w0)
                            w0 = []
                        mm(bk[:, off:off + 128], vtm[:, wl, ch, :],
                           attT[:, sl, 128:256], start=(w == 0),
                           stop=True, wait=w0,
                           inc=(s_t,) if sl == 7 else ())
                    av_done[i] = (s_t, s_t.n)

                def emit_ave(i):
                    wl, hg = whgs[i]
                    wc = slice(wl * 128, (wl + 1) * 128)
                    for half in range(2):
                        b0 = half * 64
                        act(aout[b0:b0 + 64, hg * 4:hg * 4 + 4, wc],
                            psF[4 + half][b0:b0 + 64, :].rearrange(
                                "p (a b) -> p a b", a=4), AF.Copy,
                            wait=[av_done[i]] if half == 0 else (),
                            inc=(s_s,))
                        rec_avbank[half] = (s_s, s_s.n)

                for i in range(9):
                    if i < 8:
                        emit_fill_sims(i)
                    if i >= 1:
                        emit_trans(i - 1)
                    if i < 8:
                        emit_exp(i)
                    if i >= 1:
                        emit_attTcp(i - 1)
                    if i < 8:
                        emit_sumpbf(i)
                    if i >= 1:
                        emit_avm(i - 1)
                        emit_ave(i - 1)
                vcp(kPprev[:], rkP[:, :, 384:512], wait=[(s_t, s_t.n)])
                vcp(vtmprev[:], vtm[:, 3, :, :])
                rec_op = {0: None, 1: None}
                for oc in range(NCH):
                    bk = psF[oc % 2]
                    for c in range(NCH):
                        w0 = ()
                        if c == 0:
                            w0 = ([(s_v, s_v.n), (s_s, s_s.n)]
                                  if oc == 0 else
                                  ([rec_op[oc % 2]] if rec_op[oc % 2]
                                   else []))
                        mm(bk[:], wout[:, c, oc * 128:(oc + 1) * 128],
                           aout[:, c, :], start=(c == 0), stop=(c == 7),
                           wait=w0, inc=(s_t,) if c == 7 else ())
                    vtt(x32[:, oc, scs], x32[:, oc, scs], bk[:], ALU.add,
                        wait=[(s_t, s_t.n)])
                    rec_op[oc % 2] = (s_v, s_v.n)
                if ss == 0:
                    emit_cache([k for k in ("l1_h2", "l1_w3")
                                if k not in wc_gate])
                em.barrier(exclude=em.barrier_exclude)
            emit_ln(sq_scr, outbf=True)

        def emit_l1f():
            ar.reset()
            wsl_mark = [ar.mark()]
            wsl = []
            for _ in range(2):
                wsl_mark.append(ar.mark())
                wsl.append(ar.take((128, 32, 256), BF16))
            wslA = [ar.view(wsl_mark[1], (128, 16, 512), BF16),
                    ar.view(wsl_mark[2], (128, 16, 512), BF16)]
            h_mark = ar.mark()
            h_sb = ar.take((128, 32, 512), BF16)
            g_mark = ar.mark()
            g_sb = ar.take((128, 32, 512), BF16)
            s1, s2 = smalls["l1_s1"], smalls["l1_s2"]
            sh1, sh2, s3 = (smalls["l1_sh1"], smalls["l1_sh2"],
                            smalls["l1_s3"])
            silt = ar.view(g_mark, (128, 2, 512), F32)
            sq_scr = ar.view(h_mark, (128, 8, 512), BF16)
            otm = [ar.view(g_mark + 4096 + 2048 * k, (128, 512), F32)
                   for k in range(4)]
            emit_cache([k for k in ("l1_w1", "l1_w2", "l1_h1", "l1_h2",
                                    "l1_w3") if k not in wc_gate])
            em.barrier(exclude=em.barrier_exclude)

            nD_v = s_v.n
            pend = None
            pending_store = None

            def emit_store(st):
                for tb in range(4):
                    trow = st * 512 + tb * 128
                    for cg in range(2):
                        bank = psF[4 + cg]
                        ob = otm[(tb % 2) * 2 + cg]
                        for j in range(4):
                            tr(bank[:, j * 128:(j + 1) * 128],
                               x32[:, cg * 4 + j, trow:trow + 128],
                               csb["ident32"][:],
                               wait=[(s_st, max(0, s_st.n - 32)),
                                     (s_v, s_v.n)] if j == 0 else (),
                               inc=(s_t,) if j == 3 else ())
                        vcp(ob, bank[:], wait=[(s_t, s_t.n)])
                        SY(lambda e, o=xout[trow:trow + 128,
                                            cg * 512:cg * 512 + 512],
                           i=ob: e.dma_start(out=o, in_=i),
                           wait=[(s_v, s_v.n)], inc=(s_st,))
                ln_rec["m"] = (s_v, s_v.n)
                ln_rec["v"] = (s_v, s_v.n)

            for tt in range(4):
                tc = slice(tt * 512, tt * 512 + 512)

                def slabA(gg, gate=None):
                    w = [(s_t, s_t.n)]
                    if gate is not None:
                        w = w + [(s_wc, gate)]
                    SYW2(lambda e, d=wslA[gg % 2][:, 0:8, :], o0=gg * 512:
                        e.dma_start(
                        out=d, in_=wbf["l1_w1"][:, o0:o0 + 512].rearrange(
                            "(c p) o -> p c o", p=128)), wait=w)
                    SCD(lambda e, d=wslA[gg % 2][:, 8:16, :], o0=gg * 512:
                        e.dma_start(
                        out=d, in_=wbf["l1_w2"][:, o0:o0 + 512].rearrange(
                            "(c p) o -> p c o", p=128)), wait=w)

                def slabH(gg, wn, gate=None):
                    w = [(s_t, s_t.n)]
                    if gate is not None:
                        w = w + [(s_wc, gate)]
                    SYW2(lambda e, d=wsl[gg % 2][:, 0:16, :], o0=gg * 256:
                        e.dma_start(out=d, in_=wbf[wn][
                            0:2048, o0:o0 + 256].rearrange(
                            "(c p) o -> p c o", p=128)), wait=w)
                    SCD(lambda e, d=wsl[gg % 2][:, 16:32, :], o0=gg * 256:
                        e.dma_start(out=d, in_=wbf[wn][
                            2048:4096, o0:o0 + 256].rearrange(
                            "(c p) o -> p c o", p=128)), wait=w)

                def slabD(gg, gate=None):
                    slabH(gg, "l1_w3", gate)

                gt = (lambda k: wc_gate[k] if tt == 0 else None)
                stage_w = [(s_v, nD_v)]
                if pend is None:
                    base_wa, base_wb = s_wa.n, s_wb.n
                    slabA(0, (max(wc_gate["l1_w1"], wc_gate["l1_w2"])
                              if tt == 0 else None))
                else:
                    base_wa, base_wb = pend
                    pend = None
                rec_silu = {0: (s_s, 0), 1: (s_s, 0)}
                rec_stt = {0: (s_v, 0), 1: (s_v, 0)}
                for g in range(8):
                    if g == 1 and pending_store is not None:
                        emit_store(pending_store)
                        pending_store = None
                    if g + 1 < 8:
                        slabA(g + 1)
                    else:
                        pend = (s_wa.n, s_wb.n)
                        slabH(0, "l1_h1", gt("l1_h1"))
                    w1v = wslA[g % 2]
                    for j in range(4):
                        oc = g * 4 + j
                        b = j % 2
                        bk1 = psF[b]
                        bk2 = psF[2 + b]
                        for c in range(NCH):
                            w0 = ()
                            if c == 0:
                                w0 = [(s_wa, base_wa + 16 * (g + 1)),
                                      (s_wb, base_wb + 16 * (g + 1)),
                                      rec_silu[b]]
                                if g == 0 and j == 0:
                                    w0 += stage_w
                            mm(bk1[:], w1v[:, c, j * 128:(j + 1) * 128],
                               xbf[:, c, tc], start=(c == 0), stop=(c == 7),
                               wait=w0)
                        for c in range(NCH):
                            mm(bk2[:], w1v[:, 8 + c, j * 128:(j + 1) * 128],
                               xbf[:, c, tc], start=(c == 0), stop=(c == 7),
                               wait=[rec_stt[b]] if c == 0 else (),
                               inc=(s_t,) if c == 7 else ())
                        act(silt[:, b, :], bk1[:], AF.Silu,
                            scale=s1[:, oc:oc + 1],
                            wait=[(s_t, s_t.n), rec_stt[b]], inc=(s_s,))
                        rec_silu[b] = (s_s, s_s.n)
                        vstt(h_sb[:, oc, :], bk2[:], s2[:, oc:oc + 1],
                             silt[:, b, :], ALU.mult, ALU.mult,
                             wait=[rec_silu[b]])
                        rec_stt[b] = (s_v, s_v.n)
                stage_w = [(s_v, s_v.n), (s_s, s_s.n), (s_st, s_st.n)]
                for wname, ssc, src_sb, dst_sb in (
                        ("l1_h1", sh1, h_sb, g_sb), ("l1_h2", sh2, g_sb,
                                                     h_sb)):
                    if pend is None:
                        base_wa, base_wb = s_wa.n, s_wb.n
                        slabH(0, wname, gt(wname))
                    else:
                        base_wa, base_wb = pend
                        pend = None
                    rec_gelu = {0: (s_s, 0), 1: (s_s, 0)}
                    for g in range(16):
                        if g + 1 < 16:
                            slabH(g + 1, wname)
                        else:
                            pend = (s_wa.n, s_wb.n)
                            if wname == "l1_h1":
                                slabH(0, "l1_h2", gt("l1_h2"))
                            else:
                                slabD(0, gt("l1_w3"))
                        for j in range(2):
                            oc = g * 2 + j
                            bk = psF[j]
                            for c in range(32):
                                w0 = ()
                                if c == 0:
                                    w0 = [(s_wa, base_wa + 16 * (g + 1)),
                                          (s_wb, base_wb + 16 * (g + 1)),
                                          rec_gelu[j]]
                                    if g == 0 and j == 0:
                                        w0 += stage_w
                                mm(bk[:], wsl[g % 2][:, c,
                                                     j * 128:(j + 1) * 128],
                                   src_sb[:, c, :], start=(c == 0),
                                   stop=(c == 31), wait=w0,
                                   inc=(s_t,) if c == 31 else ())
                            act(dst_sb[:, oc, :], bk[:], AF.Gelu,
                                scale=ssc[:, oc:oc + 1], wait=[(s_t, s_t.n)])
                            rec_gelu[j] = (s_s, s_s.n)
                    stage_w = [(s_s, s_s.n)]
                if pend is None:
                    base_wa, base_wb = s_wa.n, s_wb.n
                    slabD(0, gt("l1_w3"))
                else:
                    base_wa, base_wb = pend
                    pend = None
                rec_d = {0: (s_v, 0), 1: (s_v, 0)}
                for g in range(4):
                    if g + 1 < 4:
                        slabD(g + 1)
                    elif tt + 1 < 4:
                        pend = (s_wa.n, s_wb.n)
                        slabA(0)
                    for j in range(2):
                        oc = g * 2 + j
                        bk = psF[j]
                        for c in range(32):
                            w0 = ()
                            if c == 0:
                                w0 = [(s_wa, base_wa + 16 * (g + 1)),
                                      (s_wb, base_wb + 16 * (g + 1)),
                                      rec_d[j]]
                                if g == 0 and j == 0:
                                    w0 += stage_w
                            mm(bk[:], wsl[g % 2][:, c, j * 128:(j + 1) * 128],
                               h_sb[:, c, :], start=(c == 0), stop=(c == 31),
                               wait=w0, inc=(s_t,) if c == 31 else ())
                        vstt(x32[:, oc, tc], bk[:], s3[:, oc:oc + 1],
                             x32[:, oc, tc], ALU.mult, ALU.add,
                             wait=[(s_t, s_t.n)])
                        rec_d[j] = (s_v, s_v.n)
                nD_v = s_v.n
                emit_ln_tile(tt, sq_scr, outbf=False)
                if out_tok:
                    if tt + 1 < 4:
                        pending_store = tt
                    else:
                        emit_store(tt)
            em.barrier(exclude=em.barrier_exclude)

        for ph in phases:
            dict(l0a=emit_l0a, l0f=emit_l0f, l1a=emit_l1a,
                 l1f=emit_l1f)[ph]()

        if not out_tok:
            em.barrier(exclude=em.barrier_exclude)
            SY(lambda e: e.dma_start(
                out=xout.rearrange("(c p) t -> p c t", p=128), in_=x32[:]),
                inc=(s_st,))
        em.op("sync", None, [(s_st, s_st.n)], [])

        for s in em.sems:
            s.h = E(nc.semaphore(s.name))
        block = E(nc.Block())

        def make_body(items):
            def body(e):
                for _, fn, waits, incs in items:
                    for (sm, val) in waits:
                        if val > 0:
                            e.wait_ge(sm.h, val)
                    ins = fn(e) if fn is not None else None
                    if incs:
                        if ins is None:
                            ins = e.nop()
                        for (sm, k) in incs:
                            ins.then_inc(sm.h, k)
            return body

        for eng in ENGINES:
            items = [it for it in em.items if it[0] == eng]
            if items:
                getattr(block, eng)(make_body(items))

    return nc, names


_PROG_CACHE = {}


def _get_prog(phases):
    key = tuple(phases)
    if key not in _PROG_CACHE:
        _PROG_CACHE[key] = build_program(list(phases))
    return _PROG_CACHE[key]


LAUNCH_GROUPS = [("l0a", "l0f", "l1a", "l1f")]


def kernel(**inputs):
    w = host_weights(inputs)
    c = host_consts()
    x = _f32(inputs["L0_attn_x"])
    cur = [np.ascontiguousarray(x[b]) for b in range(NC)]

    for phases in LAUNCH_GROUPS:
        nc, names = _get_prog(phases)
        in_maps = []
        for b in range(NC):
            m = {}
            for nm in names:
                if nm in ("xin", "xin_fm"):
                    m[nm] = cur[b]
                elif nm in c:
                    m[nm] = c[nm]
                else:
                    m[nm] = w[nm]
            in_maps.append(m)
        res = run_bass_kernel_spmd(nc, in_maps, core_ids=list(range(NC)))
        okey = "xout" if phases[-1] == "l1f" else "xout_fm"
        cur = [np.ascontiguousarray(np.asarray(res.results[b][okey]))
               for b in range(NC)]

    return np.stack(cur, axis=0).astype(np.float32)


# version 29
# speedup vs baseline: 1.0459x; 1.0459x over previous
import contextlib
import numpy as np
import ml_dtypes
import concourse.bass as bass
import concourse.mybir as mybir
from concourse.bass_utils import run_bass_kernel_spmd
from concourse.hw_specs import get_activation_tables

F32 = mybir.dt.float32
BF16 = mybir.dt.bfloat16
AF = mybir.ActivationFunctionType
ALU = mybir.AluOpType
AX = mybir.AxisListType

NC = 8
T = 2048
D = 1024
NCH = 8
EPS = 1e-5
INVD = 1.0 / D
NEG = -3.0e38

L0_DILS = [1, 2, 4, 8]
L0_NB = 16
L0_NORM = float(1.0 / np.sqrt(128.0))

_SET_IDX = None
_FUNC_ANY = {AF.Copy, AF.Identity, AF.Relu, AF.Square, AF.Sign, AF.Abs}
_FUNC_SET = {
    AF.Exp: "natural_log_exp_and_others",
    AF.Ln: "natural_log_exp_and_others",
    AF.Sigmoid: "sigmoid_and_others",
    AF.Gelu: "gelu_and_others",
    AF.Silu: "silu_and_others",
}


def _act_set_idx(nc, name):
    global _SET_IDX
    if _SET_IDX is None:
        _SET_IDX = {n: i for i, n in
                    enumerate(get_activation_tables(nc.m.arch).keys())}
    return _SET_IDX[name]


def _f32(x):
    return np.ascontiguousarray(np.asarray(x), dtype=np.float32)


def _pc(v, nch):
    return np.ascontiguousarray(_f32(v).reshape(nch, 128).T)


def host_weights(i):
    w = {}
    w["l0_wqkv"] = np.ascontiguousarray(np.stack(
        [_f32(i["L0_attn_Wk"]), _f32(i["L0_attn_Wq"]), _f32(i["L0_attn_Wv"])],
        axis=0).transpose(0, 1, 3, 2))
    w["l0_wout"] = np.ascontiguousarray(_f32(i["L0_attn_W_out"]).T)
    w["l0_bout"] = _pc(i["L0_attn_b_out"], 8)
    w["l0f_w"] = np.ascontiguousarray(np.stack(
        [_f32(i["L0_ffn_W_r"]).T, _f32(i["L0_ffn_W_k"]).T,
         _f32(i["L0_ffn_W_v"]).T], axis=0))
    w["l0f_mur"] = _pc(i["L0_ffn_mu_r"], 8)
    w["l0f_muk"] = _pc(i["L0_ffn_mu_k"], 8)
    w["l1_wqkv"] = _f32(i["L1_attn_W_qkv"])
    w["l1_wout"] = _f32(i["L1_attn_W_out"])
    w["l1_w1"] = np.ascontiguousarray(_f32(i["L1_ffn_w1_w"]).T)
    w["l1_w2"] = np.ascontiguousarray(_f32(i["L1_ffn_w2_w"]).T)
    w["l1_h1"] = np.ascontiguousarray(_f32(i["L1_ffn_h1_w"]).T)
    w["l1_h2"] = np.ascontiguousarray(_f32(i["L1_ffn_h2_w"]).T)
    w["l1_w3"] = np.ascontiguousarray(_f32(i["L1_ffn_w3_w"]).T)
    w["l1_s1"] = _pc(i["L1_ffn_w1_s"], 32)
    w["l1_s2"] = _pc(i["L1_ffn_w2_s"], 32)
    w["l1_sh1"] = _pc(i["L1_ffn_h1_s"], 32)
    w["l1_sh2"] = _pc(i["L1_ffn_h2_s"], 32)
    w["l1_s3"] = _pc(i["L1_ffn_w3_s"], 8)
    return w


def host_consts():
    c = {}
    c["ident32"] = np.eye(128, dtype=np.float32)
    c["identbf"] = np.eye(128, dtype=ml_dtypes.bfloat16)
    c["onesbf"] = np.ones((128, 128), dtype=ml_dtypes.bfloat16)
    c["zerobf"] = np.zeros((128, 128), dtype=ml_dtypes.bfloat16)
    dh = 64
    inv = 1.0 / (10000.0 ** (np.arange(0, dh, 2, dtype=np.float64) / dh))
    pos = np.arange(256, dtype=np.float64)
    fr = pos[:, None] * inv[None, :]
    fr = np.concatenate([fr, fr], axis=1)
    sc = 1.0 / np.sqrt(dh)
    dup = lambda t: np.ascontiguousarray(
        np.concatenate([t, t], axis=0), dtype=ml_dtypes.bfloat16)
    c["ropeQc"] = dup(np.cos(fr[128:]).T * sc)
    c["ropeQs"] = dup(np.sin(fr[128:]).T * sc)
    c["ropeKc"] = dup(np.cos(fr[128:]).T)
    c["ropeKs"] = dup(np.sin(fr[128:]).T)
    c["ropePc"] = dup(np.cos(fr[:128]).T)
    c["ropePs"] = dup(np.sin(fr[:128]).T)
    P = np.zeros((128, 128), dtype=np.float32)
    for hb in range(2):
        b = hb * 64
        for j in range(32):
            P[b + j, b + 32 + j] = -1.0
            P[b + 32 + j, b + j] = 1.0
    c["pmat"] = np.ascontiguousarray(P.T.astype(ml_dtypes.bfloat16))
    ii = np.arange(128)[:, None]
    jj = np.arange(128)[None, :]
    cm = np.full((128, 256), NEG, dtype=np.float32)
    cm[:, 0:128][jj >= ii] = 0.0
    cm[:, 128:256][jj <= ii] = 0.0
    c["cm2"] = np.ascontiguousarray(
        np.concatenate([cm, cm], axis=1).astype(ml_dtypes.bfloat16))
    cw = np.full((128, 256), NEG, dtype=np.float32)
    cw[:, 128:256][jj <= ii] = 0.0
    c["cmw0"] = np.ascontiguousarray(
        np.concatenate([cw, cw], axis=1).astype(ml_dtypes.bfloat16))
    return c


ENGINES = ("sync", "gpsimd", "tensor", "vector", "scalar")


class Sem:
    __slots__ = ("name", "n", "h")

    def __init__(self, name):
        self.name = name
        self.n = 0
        self.h = None


class Em:
    def __init__(self):
        self.items = []
        self.sems = []
        self.cur_set = None
        self.auto_sems = {}
        self.barrier_exclude = ()

    def sem(self, name):
        s = Sem(name)
        self.sems.append(s)
        return s

    def op(self, eng, fn=None, wait=(), inc=(), dma=False):
        w = [(x, x.n) if isinstance(x, Sem) else x for x in wait]
        if fn is not None and not dma and eng in self.auto_sems:
            a = self.auto_sems[eng]
            w.append((a, a.n))
        ii = []
        for x in inc:
            k = 16 if dma else 1
            if isinstance(x, tuple):
                x, k = x
            x.n += k
            ii.append((x, k))
        self.items.append((eng, fn, w, ii))

    def barrier(self, exclude=(), engines=ENGINES):
        waits = [(s, s.n) for s in self.sems if s.n > 0 and s not in exclude]
        for eng in engines:
            self.items.append((eng, None, list(waits), []))


class Arena:
    def __init__(self, ap8, nbytes):
        self.ap8 = ap8
        self.nbytes = nbytes
        self.off = 0
        self.hwm = 0

    def reset(self):
        self.off = 0

    def view(self, off, shape, dtype):
        esz = 2 if dtype == BF16 else 4
        n = int(np.prod(shape[1:])) * esz
        assert off + n <= self.nbytes
        v = self.ap8[:, off:off + n].bitcast(dtype)
        if len(shape) == 2:
            return v
        dims = "abcdef"[:len(shape) - 1]
        pat = "p (" + " ".join(dims) + ") -> p " + " ".join(dims)
        kw = {d: shape[1 + i] for i, d in enumerate(dims)}
        return v.rearrange(pat, **kw)

    def mark(self):
        return (self.off + 63) & ~63

    def take(self, shape, dtype):
        esz = 2 if dtype == BF16 else 4
        n = int(np.prod(shape[1:])) * esz
        off = (self.off + 63) & ~63
        assert off + n <= self.nbytes, f"arena overflow {off}+{n}>{self.nbytes}"
        self.off = off + n
        self.hwm = max(self.hwm, self.off)
        return self.view(off, shape, dtype)


def build_program(phases):
    first, last = phases[0], phases[-1]
    in_tok = first == "l0a"
    out_tok = last == "l1f"

    nc = bass.Bass("TRN2", target_bir_lowering=False)
    em = Em()
    names = []

    def din(name, shape, dtype=F32):
        names.append(name)
        return nc.dram_tensor(name, list(shape), dtype,
                              kind="ExternalInput").ap()

    import os
    dbg_cat = os.environ.get("K_DBG_CAT") == "1" and phases == ["l0a"]
    dbg_pre = os.environ.get("K_DBG_PRE") == "1" and phases == ["l0a"]
    dbg_att = os.environ.get("K_DBG_ATT") == "1" and phases == ["l0a"]
    l1a_lvl = int(os.environ.get("K_L1A_LVL", "3"))
    xin = din("xin" if in_tok else "xin_fm", (T, D) if in_tok else (D, T))
    dbg_out = (nc.dram_tensor("dbg_cat", [512, T], BF16,
                              kind="ExternalOutput").ap() if dbg_cat else None)
    dbg_pre_out = (nc.dram_tensor("dbg_pre", [D, T], F32,
                                  kind="ExternalOutput").ap() if dbg_pre
                   else None)
    dbg_att_out = (nc.dram_tensor("dbg_att", [4, 128, 16, 128], F32,
                                  kind="ExternalOutput").ap() if dbg_att
                   else None)
    dbg_smd_out = (nc.dram_tensor("dbg_smd", [4, 128, 256], F32,
                                  kind="ExternalOutput").ap() if dbg_att
                   else None)
    dbg_sm2_out = (nc.dram_tensor("dbg_sm2", [4, 128, 576], F32,
                                  kind="ExternalOutput").ap() if dbg_att
                   else None)
    dbg_kqv_out = (nc.dram_tensor("dbg_kqv", [4, 128, 16, 384], F32,
                                  kind="ExternalOutput").ap() if dbg_att
                   else None)
    xout = nc.dram_tensor("xout" if out_tok else "xout_fm",
                          [T, D] if out_tok else [D, T], F32,
                          kind="ExternalOutput").ap()

    cshapes = dict(ident32=((128, 128), F32), identbf=((128, 128), BF16),
                   onesbf=((128, 128), BF16), zerobf=((128, 128), BF16),
                   ropeQc=((128, 128), BF16), ropeQs=((128, 128), BF16),
                   ropeKc=((128, 128), BF16), ropeKs=((128, 128), BF16),
                   ropePc=((128, 128), BF16), ropePs=((128, 128), BF16),
                   pmat=((128, 128), BF16), cm2=((128, 512), BF16),
                   cmw0=((128, 512), BF16))
    need_c = {"ident32", "identbf", "onesbf"}
    if "l0a" in phases:
        need_c |= {"zerobf"}
    if "l1a" in phases:
        need_c |= {"ropeQc", "ropeQs", "ropeKc", "ropeKs", "ropePc", "ropePs",
                   "pmat", "cm2", "cmw0"}
    wshapes = dict(l0_wqkv=(3, 4, 1024, 128), l0_wout=(512, 1024),
                   l0_bout=(128, 8), l0f_w=(3, 1024, 1024),
                   l0f_mur=(128, 8), l0f_muk=(128, 8),
                   l1_wqkv=(1024, 3072), l1_wout=(1024, 1024),
                   l1_w1=(1024, 4096), l1_w2=(1024, 4096),
                   l1_h1=(4096, 4096), l1_h2=(4096, 4096), l1_w3=(4096, 1024),
                   l1_s1=(128, 32), l1_s2=(128, 32), l1_sh1=(128, 32),
                   l1_sh2=(128, 32), l1_s3=(128, 8))
    need_w = set()
    if "l0a" in phases:
        need_w |= {"l0_wqkv", "l0_wout", "l0_bout"}
    if "l0f" in phases:
        need_w |= {"l0f_w", "l0f_mur", "l0f_muk"}
    if "l1a" in phases:
        need_w |= {"l1_wqkv", "l1_wout"}
    if "l1f" in phases:
        need_w |= {"l1_w1", "l1_w2", "l1_h1", "l1_h2", "l1_w3",
                   "l1_s1", "l1_s2", "l1_sh1", "l1_sh2", "l1_s3"}
    cst = {k: din(k, *cshapes[k]) for k in sorted(need_c)}
    wdr = {k: din(k, wshapes[k]) for k in sorted(need_w)}
    wbf = {}
    if "l1f" in phases:
        for k in ("l1_w1", "l1_w2", "l1_h1", "l1_h2", "l1_w3"):
            wbf[k] = nc.dram_tensor("bfc_" + k, list(wshapes[k]), BF16,
                                    kind="Internal").ap()
    if "l1a" in phases:
        wbf["l1_wqkv"] = nc.dram_tensor("bfc_l1_wqkv",
                                        list(wshapes["l1_wqkv"]), BF16,
                                        kind="Internal").ap()

    with contextlib.ExitStack() as ctx:
        E = ctx.enter_context
        x32 = E(nc.sbuf_tensor("x32", [128, NCH, T], F32))
        xbf = E(nc.sbuf_tensor("xbf", [128, NCH, T], BF16))
        csb = {k: E(nc.sbuf_tensor("c_" + k, list(cshapes[k][0]),
                                   cshapes[k][1])) for k in sorted(need_c)}
        smalls = {k: E(nc.sbuf_tensor("w_" + k, list(wshapes[k]), F32))
                  for k in sorted(need_w) if wshapes[k][0] == 128}
        ln_m = E(nc.sbuf_tensor("ln_m", [128, 512], F32))
        epsb = E(nc.sbuf_tensor("epsb", [128, 1], F32))
        junk = E(nc.sbuf_tensor("junk", [128, 8], F32))
        ln_r = E(nc.sbuf_tensor("ln_r", [128, 512], F32))
        ln_t = E(nc.sbuf_tensor("ln_t", [128, 512], F32))
        AB = 98 * 1024
        ar8 = E(nc.sbuf_tensor("arena", [128, AB], mybir.dt.uint8))
        ar = Arena(ar8[:], AB)
        psF6 = E(nc.psum_tensor("psF6", [128, 6, 512], F32))
        psB2 = E(nc.psum_tensor("psB2", [128, 2, 1024], BF16))
        psF = [psF6[:, i, :] for i in range(6)]
        psB = [psB2[:, i, :] for i in range(2)]

        s_ld = em.sem("s_ld")
        s_ti = [em.sem(f"s_ti{k}") for k in range(4)]
        s_wl = em.sem("s_wl")
        s_t = em.sem("s_t")
        s_v = em.sem("s_v")
        s_s = em.sem("s_s")
        s_g = em.sem("s_g")
        s_st = em.sem("s_st")
        s_wa = em.sem("s_wa")
        s_wb = em.sem("s_wb")
        s_wc = em.sem("s_wc")
        em.auto_sems = {"vector": s_v, "scalar": s_s, "gpsimd": s_g}
        em.barrier_exclude = (s_wc,)

        def SYW2(fn, wait=(), inc=(s_wa,)):
            em.op("sync", fn, wait, inc, dma=True)

        def SCD(fn, wait=(), inc=(s_wb,)):
            em.op("scalar", fn, wait, inc, dma=True)

        def SY(fn, wait=(), inc=(s_ld,)):
            em.op("sync", fn, wait, inc, dma=True)

        def GD(fn, wait=(), inc=(s_wl,)):
            em.op("gpsimd", fn, wait, inc, dma=True)

        def mm(out, lhsT, rhs, start, stop, wait=(), inc=()):
            em.op("tensor", lambda e: e.matmul(out, lhsT, rhs, start=start,
                                               stop=stop), wait, inc)

        def tr(out, in_, ident, wait=(), inc=()):
            em.op("tensor", lambda e: e.transpose(out, in_, ident), wait, inc)

        def twait(wait):
            em.op("tensor", None, wait, ())

        def vtt(out, in0, in1, alu, wait=(), inc=(s_v,), eng="vector"):
            em.op(eng, lambda e: e.tensor_tensor(out=out, in0=in0, in1=in1,
                                                 op=alu), wait, inc)

        def vts(out, in0, s1, alu, wait=(), inc=(s_v,), eng="vector"):
            em.op(eng, lambda e: e.tensor_single_scalar(out=out, in_=in0,
                                                        scalar=s1, op=alu),
                  wait, inc)

        def vstt(out, in0, scal, in1, op0, op1, wait=(), inc=(s_v,)):
            em.op("vector", lambda e: e.scalar_tensor_tensor(
                out=out, in0=in0, scalar=scal, in1=in1, op0=op0, op1=op1),
                wait, inc)

        def vcp(out, in_, wait=(), inc=(s_v,), eng="vector"):
            em.op(eng, lambda e: e.tensor_copy(out=out, in_=in_), wait, inc)

        def vred(out, in_, alu, wait=(), inc=(s_v,)):
            em.op("vector", lambda e: e.tensor_reduce(out=out, in_=in_,
                                                      axis=AX.X, op=alu),
                  wait, inc)

        def vrecip(out, in_, wait=(), inc=(s_v,)):
            em.op("vector", lambda e: e.reciprocal(out=out, in_=in_),
                  wait, inc)

        def vmset(ap, val, wait=(), inc=(s_v,), eng="vector"):
            em.op(eng, lambda e: e.memset(ap, val), wait, inc)

        def vsettle(eng="vector"):
            sems = {"vector": s_v, "scalar": s_s, "gpsimd": s_g}
            if eng == "scalar":
                em.op(eng, lambda e: e.memzero(junk[:, 0:1]), (),
                      (sems[eng],))
            else:
                em.op(eng, lambda e: e.memset(junk[:, 0:1], 0.0), (),
                      (sems[eng],))

        def act(out, in_, func, bias=0.0, scale=1.0, accum_out=None,
                wait=(), inc=(s_s,)):
            if func not in _FUNC_ANY:
                sname = _FUNC_SET[func]
                if em.cur_set != sname:
                    sid = _act_set_idx(nc, sname)
                    em.op("scalar", lambda e, sid=sid: e.add_instruction(
                        mybir.InstLoadActFuncSet(
                            name=e.bass.get_next_instruction_name(),
                            act_func_set_id=sid, ins=[], outs=[])))
                    em.cur_set = sname
            em.op("scalar", lambda e: e.activation(
                out=out, in_=in_, func=func, bias=bias, scale=scale,
                accum_out=accum_out), wait, inc)

        for k in sorted(need_c):
            SY(lambda e, d=csb[k], s=cst[k]: e.dma_start(out=d[:], in_=s))
        for k in sorted(smalls):
            SY(lambda e, d=smalls[k], s=wdr[k]: e.dma_start(out=d[:], in_=s))

        vmset_done = em.op("vector", lambda e: e.memset(epsb[:], EPS), (), ((s_v, 1),))
        if in_tok:
            tin = ar.view(34 * 1024, (128, 16, 1024), F32)
            xin_r = xin.rearrange("(tb p) c -> p tb c", p=128)
            for k in range(4):
                SY(lambda e, d=tin[:, 4 * k:4 * k + 4, :],
                   s=xin_r[:, 4 * k:4 * k + 4, :]:
                   e.dma_start(out=d, in_=s), inc=(s_ti[k],))
            rec_in = {0: None, 1: None}
            for tb in range(16):
                for cg in range(2):
                    bank = psF[cg % 2]
                    for j in range(4):
                        w0 = ()
                        if j == 0:
                            w0 = [(s_ti[tb // 4], 16)]
                            if tb == 0 and cg == 0:
                                w0 = w0 + [(s_ld, s_ld.n)]
                            if rec_in[cg % 2]:
                                w0 = w0 + [rec_in[cg % 2]]
                        tr(bank[:, j * 128:(j + 1) * 128],
                           tin[:, tb, (cg * 4 + j) * 128:
                               (cg * 4 + j) * 128 + 128],
                           csb["ident32"][:],
                           wait=w0,
                           inc=(s_t,) if j == 3 else ())
                    vcp(x32[:, cg * 4:cg * 4 + 4, tb * 128:(tb + 1) * 128],
                        bank[:].rearrange("p (a b) -> p a b", a=4),
                        wait=[(s_t, s_t.n)])
                    vcp(xbf[:, cg * 4:cg * 4 + 4, tb * 128:(tb + 1) * 128],
                        bank[:].rearrange("p (a b) -> p a b", a=4))
                    rec_in[cg % 2] = (s_v, s_v.n)
            ar.reset()
        else:
            SY(lambda e: e.dma_start(
                out=x32[:], in_=xin.rearrange("(c p) t -> p c t", p=128)))
            GD(lambda e: e.dma_start(
                out=xbf[:], in_=xin.rearrange("(c p) t -> p c t", p=128)))
        wc_gate = {}

        def emit_cache(keys):
            keys = [k for k in keys if k in wbf]
            for k in keys:
                kk = wdr[k].shape[0]
                for half in range(2 if kk >= 4096 else 1):
                    hh = kk // 2 if kk >= 4096 else kk
                    GD(lambda e, d=wbf[k][half * hh:(half + 1) * hh, :],
                        src=wdr[k][half * hh:(half + 1) * hh, :]:
                        e.dma_start(out=d, in_=src), inc=(s_wc,))
                wc_gate[k] = s_wc.n
        em.barrier(exclude=em.barrier_exclude)

        def emit_ln(sq_scr, outbf):
            base = [(s_v, s_v.n), (s_s, s_s.n)]
            for tt in range(4):
                emit_ln_tile(tt, sq_scr, outbf, base=base)

        ln_rec = {"m": (s_v, 0), "v": (s_v, 0), "ap": (s_v, 0),
                  "sq": (s_t, 0)}

        def emit_ln_tile(tt, sq_scr, outbf, base=None):
            tc = slice(tt * 512, tt * 512 + 512)
            ybf = xbf[:, :, tc]
            act(sq_scr, x32[:, :, tc], AF.Square,
                wait=([base[0], ln_rec["sq"]] if base
                      else [(s_v, s_v.n)]), inc=())
            vsettle("scalar")
            vcp(ybf, x32[:, :, tc], inc=())
            vsettle()
            twait([(s_v, s_v.n), (s_s, s_s.n)])
            for c in range(NCH):
                mm(psF[4][:], csb["onesbf"][:], ybf[:, c, :],
                   start=(c == 0), stop=(c == 7),
                   wait=[ln_rec["m"]] if c == 0 else (),
                   inc=(s_t,) if c == 7 else ())
            for c in range(NCH):
                mm(psF[5][:], csb["onesbf"][:], sq_scr[:, c, :],
                   start=(c == 0), stop=(c == 7),
                   wait=[ln_rec["v"]] if c == 0 else (),
                   inc=(s_t,) if c == 7 else ())
            ln_rec["sq"] = (s_t, s_t.n)
            vts(ln_m[:], psF[4][:], INVD, ALU.mult,
                wait=[(s_t, s_t.n), ln_rec["ap"]])
            ln_rec["m"] = (s_v, s_v.n)
            vtt(ln_t[:], ln_m[:], ln_m[:], ALU.mult)
            vstt(ln_t[:], psF[5][:], INVD, ln_t[:], ALU.mult, ALU.subtract)
            ln_rec["v"] = (s_v, s_v.n)
            act(ln_t[:], ln_t[:], AF.Ln, bias=epsb[:], wait=[(s_v, s_v.n)])
            act(ln_r[:], ln_t[:], AF.Exp, scale=-0.5, inc=(s_s,))
            mb = ln_m[:].rearrange("p (a b) -> p a b", a=1
                                   ).broadcast_to((128, NCH, 512))
            rb = ln_r[:].rearrange("p (a b) -> p a b", a=1
                                   ).broadcast_to((128, NCH, 512))
            x3d = x32[:, :, tc]
            vtt(x3d, x3d, mb, ALU.subtract, wait=[(s_s, s_s.n)])
            vtt(x3d, x3d, rb, ALU.mult)
            ln_rec["ap"] = (s_v, s_v.n)
            if outbf:
                vcp(xbf[:, :, tc], x3d)

        def emit_l0a():
            ar.reset()
            wqkv = ar.take((128, 3, 4, 8, 128), BF16)
            wout = ar.take((128, 4, 1024), BF16)
            kqv_mark = ar.mark()
            kqv = ar.take((128, 16, 384), BF16)
            tmpd = ar.take((128, 16, 128), BF16)
            kqd = ar.take((128, 16, 16), F32)
            smdbf = ar.take((128, 16, 16), BF16)
            mx = ar.take((128, 16), F32)
            nb = ar.take((128, 16), F32)
            ssum = ar.take((128, 16), F32)
            rr = ar.take((128, 16), F32)
            att = ar.take((128, 16, 128), F32)
            attbf = ar.take((128, 16, 128), BF16)
            cat = ar.take((128, 4, 2048), BF16)
            sq_scr = ar.view(kqv_mark, (128, 8, 512), BF16)
            GD(lambda e: e.dma_start(
                out=wqkv, in_=wdr["l0_wqkv"].rearrange(
                    "a b (c p) k -> p a b c k", p=128)))
            GD(lambda e: e.dma_start(
                out=wout, in_=wdr["l0_wout"].rearrange("(v p) o -> p v o",
                                                       p=128)))
            vmset(cat, 0.0)
            emit_cache(("l1_wqkv", "l1_w1", "l1_w2"))
            em.barrier(exclude=em.barrier_exclude)

            evac_hist = []
            rec_scatter = (s_v, 0)
            rec_exp = (s_s, 0)
            for h, dil in enumerate(L0_DILS):
                L = L0_NB // dil
                ngroups = (L + 3) // 4
                for g in range(ngroups):
                    gs = min(4, L - g * 4)
                    bks = [psF[(g % 2) * 3 + j] for j in range(3)]
                    for j in range(gs):
                        n = g * 4 + j
                        Bn = n * dil
                        for c in range(NCH):
                            for proj in range(3):
                                w0 = ()
                                if (c == 0 and j == 0 and proj == 0 and
                                        len(evac_hist) >= 2):
                                    w0 = [(s_v, evac_hist[-2])]
                                mm(bks[proj][:, j * 128:(j + 1) * 128],
                                   xbf[:, c, Bn * 128:(Bn + 1) * 128],
                                   wqkv[:, proj, h, c, :],
                                   start=(c == 0), stop=(c == 7),
                                   wait=w0,
                                   inc=(s_t,) if (c == 7 and proj == 2 and
                                                  j == gs - 1) else ())
                    wt_ = [(s_t, s_t.n)]
                    for proj in range(3):
                        vcp(kqv[:, g * 4:g * 4 + gs,
                                proj * 128:(proj + 1) * 128],
                            bks[proj][:, 0:gs * 128].rearrange(
                                "p (a b) -> p a b", a=gs),
                            wait=wt_ if proj == 0 else ())
                    evac_hist.append(s_v.n)
                vmset(kqd[:, 0:L, :], NEG, wait=[rec_exp])
                for n in range(L):
                    vtt(tmpd[:, 0:n + 1, :],
                        kqv[:, n:n + 1, 0:128].broadcast_to((128, n + 1, 128)),
                        kqv[:, 0:n + 1, 128:256], ALU.mult)
                    vred(kqd[:, n, 0:n + 1], tmpd[:, 0:n + 1, :], ALU.add)
                k3 = kqd[:, 0:L, :]
                vred(mx[:, 0:L], k3, ALU.max)
                vts(nb[:, 0:L], mx[:, 0:L], -1.0, ALU.mult)
                vtt(k3, k3, nb[:, 0:L].rearrange("p (a b) -> p a b", b=1
                                                 ).broadcast_to((128, L, 16)),
                    ALU.add)
                act(k3, k3, AF.Exp, scale=L0_NORM, wait=[(s_v, s_v.n)])
                rec_exp = (s_s, s_s.n)
                vred(ssum[:, 0:L], k3, ALU.add, wait=[rec_exp])
                vrecip(rr[:, 0:L], ssum[:, 0:L])
                vtt(smdbf[:, 0:L, :], k3,
                    rr[:, 0:L].rearrange("p (a b) -> p a b", b=1
                                         ).broadcast_to((128, L, 16)),
                    ALU.mult)
                vmset(att[:, 0:L, :], 0.0)
                for n in range(L):
                    vtt(tmpd[:, 0:n + 1, :],
                        kqv[:, n:n + 1, 256:384].broadcast_to(
                            (128, n + 1, 128)),
                        smdbf[:, n, 0:n + 1].rearrange(
                            "p (a b) -> p a b", b=1).broadcast_to(
                            (128, n + 1, 128)), ALU.mult)
                    vtt(att[:, 0:n + 1, :], att[:, 0:n + 1, :],
                        tmpd[:, 0:n + 1, :], ALU.add)
                vcp(attbf[:, 0:L, :], att[:, 0:L, :], inc=())
                vsettle()
                twait([(s_v, s_v.n)])
                for m in range(L):
                    tr(psB[(m // 8) % 2][:, (m % 8) * 128:(m % 8) * 128 + 128],
                       attbf[:, m, :], csb["identbf"][:],
                       wait=[rec_scatter] if m == 0 else (),
                       inc=(s_t,) if (m % 8 == 7 or m == L - 1) else ())
                wt_ = [(s_t, s_t.n)]
                catv = cat[:, h, :].rearrange("p (s x) -> p s x", x=16)
                for m in range(L):
                    vtt(catv[:, :, m * dil],
                        psB[(m // 8) % 2][:, (m % 8) * 128:(m % 8) * 128 + 128],
                        csb["zerobf"][:], ALU.add, wait=wt_ if m == 0 else ())
                rec_scatter = (s_v, s_v.n)
            em.barrier(exclude=em.barrier_exclude)
            rec_op = {0: None, 1: None}
            first_op = True
            for tt in range(4):
                tc = slice(tt * 512, tt * 512 + 512)
                for oc in range(NCH):
                    bk = psF[oc % 2]
                    for cv in range(4):
                        w0 = ()
                        if cv == 0:
                            w0 = ([(s_v, s_v.n)] if first_op else
                                  ([rec_op[oc % 2]] if rec_op[oc % 2]
                                   else []))
                            first_op = False
                        mm(bk[:], wout[:, cv, oc * 128:(oc + 1) * 128],
                           cat[:, cv, tc], start=(cv == 0), stop=(cv == 3),
                           wait=w0, inc=(s_t,) if cv == 3 else ())
                    vstt(x32[:, oc, tc], bk[:], smalls["l0_bout"][:, oc:oc + 1],
                         x32[:, oc, tc], ALU.add, ALU.add,
                         wait=[(s_t, s_t.n)])
                    rec_op[oc % 2] = (s_v, s_v.n)
            em.barrier(exclude=em.barrier_exclude)
            if dbg_cat:
                SY(lambda e: e.dma_start(
                    out=dbg_out.rearrange("(v p) t -> p v t", p=128),
                    in_=cat[:]),
                    wait=[(s_v, s_v.n), (s_s, s_s.n), (s_t, s_t.n)],
                    inc=(s_st,))
                em.barrier(exclude=em.barrier_exclude)
            if dbg_pre:
                SY(lambda e: e.dma_start(
                    out=dbg_pre_out.rearrange("(c p) t -> p c t", p=128),
                    in_=x32[:]),
                    wait=[(s_v, s_v.n), (s_s, s_s.n), (s_t, s_t.n)],
                    inc=(s_st,))
                em.barrier(exclude=em.barrier_exclude)
            emit_ln(sq_scr, outbf=False)

        def emit_l0f():
            ar.reset()
            wf = ar.take((128, 3, 8, 1024), BF16)
            d_mark = ar.mark()
            d32 = ar.take((128, 8, 512), BF16)
            xr2 = [ar.take((128, 8, 512), BF16) for _ in range(2)]
            xk2 = [ar.take((128, 8, 512), BF16) for _ in range(2)]
            rbf = ar.take((128, 8, 512), BF16)
            tprod = ar.take((128, 1, 512), F32)
            sq_scr = ar.view(d_mark, (128, 8, 512), BF16)
            mur = smalls["l0f_mur"]
            muk = smalls["l0f_muk"]
            GD(lambda e: e.dma_start(
                out=wf, in_=wdr["l0f_w"].rearrange("a (c p) o -> p a c o",
                                                   p=128)),
                wait=[(s_v, s_v.n), (s_s, s_s.n), (s_t, s_t.n)])
            emit_cache(("l1_h1",))
            em.barrier(exclude=em.barrier_exclude)
            TQ = 512
            rec_free = {0: (s_t, 0), 1: (s_t, 0)}
            rec_bank = {0: (s_v, 0), 1: (s_v, 0)}
            prep_done = {}

            def prep(q):
                t0 = q * TQ
                xr = xr2[q % 2]
                xk = xk2[q % 2]
                if q == 0:
                    vtt(d32[:, :, 1:TQ], x32[:, :, 1:TQ],
                        x32[:, :, 0:TQ - 1], ALU.subtract,
                        wait=[rec_free[q % 2]])
                    vcp(d32[:, :, 0:1], x32[:, :, 0:1])
                else:
                    vtt(d32[:], x32[:, :, t0:t0 + TQ],
                        x32[:, :, t0 - 1:t0 + TQ - 1], ALU.subtract,
                        wait=[rec_free[q % 2]])
                for c in range(NCH):
                    lo = 1 if q == 0 else 0
                    vstt(xr[:, c, lo:TQ], d32[:, c, lo:TQ],
                         mur[:, c:c + 1],
                         x32[:, c, t0 + lo - 1:t0 + TQ - 1],
                         ALU.mult, ALU.add)
                    vstt(xk[:, c, lo:TQ], d32[:, c, lo:TQ],
                         muk[:, c:c + 1],
                         x32[:, c, t0 + lo - 1:t0 + TQ - 1],
                         ALU.mult, ALU.add)
                if q == 0:
                    mb = mur[:].rearrange("p (a b) -> p a b", b=1)
                    kb = muk[:].rearrange("p (a b) -> p a b", b=1)
                    vtt(xr[:, :, 0:1], d32[:, :, 0:1], mb, ALU.mult)
                    vtt(xk[:, :, 0:1], d32[:, :, 0:1], kb, ALU.mult)
                prep_done[q] = (s_v, s_v.n)

            prep(0)
            for q in range(T // TQ):
                t0 = q * TQ
                tc = slice(t0, t0 + TQ)
                xr = xr2[q % 2]
                xk = xk2[q % 2]
                kact = xr2[q % 2]
                if q + 1 < T // TQ:
                    prep(q + 1)
                for oc in range(NCH):
                    bk = psF[oc % 2]
                    for c in range(NCH):
                        w0 = ()
                        if c == 0:
                            w0 = [prep_done[q], rec_bank[oc % 2]]
                        mm(bk[:], wf[:, 0, c, oc * 128:(oc + 1) * 128],
                           xr[:, c, :], start=(c == 0), stop=(c == 7),
                           wait=w0, inc=(s_t,) if c == 7 else ())
                    act(rbf[:, oc, :], bk[:], AF.Sigmoid,
                        wait=[(s_t, s_t.n)])
                    rec_bank[oc % 2] = (s_s, s_s.n)
                for oc in range(NCH):
                    bk = psF[oc % 2]
                    for c in range(NCH):
                        w0 = ()
                        if c == 0:
                            w0 = [rec_bank[oc % 2]]
                        mm(bk[:], wf[:, 1, c, oc * 128:(oc + 1) * 128],
                           xk[:, c, :], start=(c == 0), stop=(c == 7),
                           wait=w0, inc=(s_t,) if c == 7 else ())
                    vts(ln_t[:], bk[:], 0.0, ALU.max,
                        wait=[(s_t, s_t.n)])
                    vtt(kact[:, oc, :], ln_t[:], ln_t[:], ALU.mult)
                    rec_bank[oc % 2] = (s_v, s_v.n)
                kact_done = (s_v, s_v.n)
                for oc in range(NCH):
                    bk = psF[oc % 2]
                    for c in range(NCH):
                        w0 = ()
                        if c == 0:
                            w0 = [kact_done] if oc == 0 else []
                            w0 = list(w0) + [rec_bank[oc % 2]]
                        mm(bk[:], wf[:, 2, c, oc * 128:(oc + 1) * 128],
                           kact[:, c, :], start=(c == 0), stop=(c == 7),
                           wait=w0, inc=(s_t,) if c == 7 else ())
                    vtt(tprod[:, 0, :], rbf[:, oc, :], bk[:],
                        ALU.mult, wait=[(s_t, s_t.n)])
                    vtt(x32[:, oc, tc], x32[:, oc, tc],
                        tprod[:, 0, :], ALU.add)
                    rec_bank[oc % 2] = (s_v, s_v.n)
                rec_free[q % 2] = (s_t, s_t.n)
            em.barrier(exclude=em.barrier_exclude)
            emit_ln(sq_scr, outbf=True)

        def emit_l1a():
            ar.reset()
            wout = ar.take((128, 8, 1024), BF16)
            wsl = [ar.take((128, 8, 512), BF16) for _ in range(2)]
            qk_mark = ar.mark()
            qk = ar.take((128, 16, 512), BF16)
            vraw_mark = ar.mark()
            vraw = ar.take((128, 8, 512), BF16)
            rkP = ar.take((128, 8, 512), BF16)
            kPprev = ar.take((128, 8, 128), BF16)
            vtm_mark = ar.mark()
            vtm = ar.take((128, 4, 8, 128), BF16)
            vtmprev = ar.take((128, 8, 128), BF16)
            simsb_mark = ar.mark()
            simsb = ar.take((128, 8, 256), F32)
            pexp2 = [ar.view(simsb_mark, (128, 8, 256), BF16),
                     ar.view(simsb_mark + 4096, (128, 8, 256), BF16)]
            aout_mark = ar.mark()
            aout = ar.take((128, 8, 512), BF16)
            pq_all = ar.view(aout_mark, (128, 8, 512), BF16)
            pk_all = ar.view(simsb_mark, (128, 8, 512), BF16)
            rtmp = ar.view(vtm_mark, (128, 2048), F32)
            pbf = ar.view(vraw_mark, (128, 8, 256), BF16)
            attT = ar.view(vraw_mark + 4096, (128, 8, 256), BF16)
            sq_scr = ar.view(qk_mark, (128, 8, 512), BF16)
            mx = ln_t[:, 0:8]
            nb8 = ln_t[:, 8:16]
            ssum = ln_t[:, 16:24]
            rr = ln_t[:, 24:32]
            GD(lambda e: e.dma_start(
                out=wout, in_=wdr["l1_wout"].rearrange("(c p) o -> p c o",
                                                       p=128)),
                wait=[(s_v, s_v.n), (s_s, s_s.n), (s_t, s_t.n)])
            em.barrier(exclude=em.barrier_exclude)
            t13 = rtmp[:].bitcast(BF16).rearrange(
                "p (a b c) -> p a b c", a=8, b=4, c=128)

            def tbl(name):
                return csb[name][:].rearrange("p (a b) -> p a b", a=1
                                              ).broadcast_to((128, 4, 128))

            SLOT8 = (0, 2, 4, 6, 1, 3, 5, 7)
            rec_avbank = [(s_s, 0)] * 2
            rec_pbf = (s_t, 0)
            exp_rec = [(s_s, 0)] * 4
            pbf_done = {-1: (s_v, 0)}
            sim_done = {}
            attT_done = {}
            av_done = {}

            for ss in range(4):
                c0 = ss * 512
                scs = slice(c0, c0 + 512)
                base_wa, base_wb = s_wa.n, s_wb.n

                def qslab(gg, gate=None):
                    w = [(s_t, s_t.n)]
                    if gate is not None:
                        w = w + [(s_wc, gate)]
                    if ss == 0:
                        GD(lambda e, d=wsl[gg % 2][:, 0:4, :], o0=gg * 512:
                            e.dma_start(out=d, in_=wdr["l1_wqkv"][
                                0:512, o0:o0 + 512].rearrange(
                                "(c p) o -> p c o", p=128)),
                            wait=w, inc=(s_wa,))
                        GD(lambda e, d=wsl[gg % 2][:, 4:8, :], o0=gg * 512:
                            e.dma_start(out=d, in_=wdr["l1_wqkv"][
                                512:1024, o0:o0 + 512].rearrange(
                                "(c p) o -> p c o", p=128)),
                            wait=w, inc=(s_wb,))
                        return
                    SYW2(lambda e, d=wsl[gg % 2][:, 0:4, :], o0=gg * 512:
                        e.dma_start(out=d, in_=wbf["l1_wqkv"][
                            0:512, o0:o0 + 512].rearrange(
                            "(c p) o -> p c o", p=128)), wait=w)
                    SCD(lambda e, d=wsl[gg % 2][:, 4:8, :], o0=gg * 512:
                        e.dma_start(out=d, in_=wbf["l1_wqkv"][
                            512:1024, o0:o0 + 512].rearrange(
                            "(c p) o -> p c o", p=128)), wait=w)
                qslab(0, wc_gate["l1_wqkv"] if ss == 1 else None)
                rec_qbank = [(s_v, 0), (s_s, 0)]
                for g in range(6):
                    if g + 1 < 6:
                        qslab(g + 1)
                    for j in range(4):
                        oc = g * 4 + j
                        bk = psF[oc % 2]
                        for c in range(NCH):
                            w0 = ()
                            if c == 0:
                                w0 = [(s_wa, base_wa + 16 * (g + 1)),
                                      (s_wb, base_wb + 16 * (g + 1)),
                                      rec_qbank[oc % 2]]
                            mm(bk[:], wsl[g % 2][:, c, j * 128:(j + 1) * 128],
                               xbf[:, c, scs], start=(c == 0), stop=(c == 7),
                               wait=w0, inc=(s_t,) if c == 7 else ())
                        dst = (qk[:, oc, :] if oc < 16 else
                               vraw[:, oc - 16, :])
                        if oc % 2 == 0:
                            vcp(dst, bk[:], wait=[(s_t, s_t.n)])
                            rec_qbank[0] = (s_v, s_v.n)
                        else:
                            act(dst, bk[:], AF.Copy, wait=[(s_t, s_t.n)])
                            rec_qbank[1] = (s_s, s_s.n)
                n_qkv_v, n_qkv_s = s_v.n, s_s.n
                for r in range(4):
                    c0r = 2 * r
                    w0 = ([(s_s, s_s.n)] if r > 0 else
                          [(s_v, n_qkv_v), (s_s, n_qkv_s)])
                    mm(psF[2][:], csb["pmat"][:], qk[:, c0r, :],
                       start=True, stop=True, wait=w0)
                    mm(psF[3][:], csb["pmat"][:], qk[:, c0r + 1, :],
                       start=True, stop=True)
                    mm(psF[4][:], csb["pmat"][:], qk[:, 8 + c0r, :],
                       start=True, stop=True)
                    mm(psF[5][:], csb["pmat"][:], qk[:, 9 + c0r, :],
                       start=True, stop=True, inc=(s_t,))
                    act(pq_all[:, c0r:c0r + 2, :], psF6[:, 2:4, :], AF.Copy,
                        wait=[(s_t, s_t.n)], inc=())
                    act(pk_all[:, c0r:c0r + 2, :], psF6[:, 4:6, :], AF.Copy,
                        inc=(s_s,))
                n_rope_s = s_s.n
                q4 = qk[:, 0:8, :].rearrange("p a (b c) -> p a b c", c=128)
                k4 = qk[:, 8:16, :].rearrange("p a (b c) -> p a b c", c=128)
                rp4 = rkP[:].rearrange("p a (b c) -> p a b c", c=128)
                pq4 = pq_all[:].rearrange("p a (b c) -> p a b c", c=128)
                pk4 = pk_all[:].rearrange("p a (b c) -> p a b c", c=128)

                def tbl4(name):
                    return csb[name][:].rearrange(
                        "p (a b c) -> p a b c", a=1, b=1).broadcast_to(
                        (128, 8, 4, 128))
                wt_ = [(s_s, n_rope_s)]
                vtt(t13, pq4, tbl4("ropeQs"), ALU.mult, wait=wt_)
                vtt(q4, q4, tbl4("ropeQc"), ALU.mult)
                vtt(q4, q4, t13, ALU.add)
                vtt(t13, pk4, tbl4("ropePs"), ALU.mult)
                vtt(rp4, k4, tbl4("ropePc"), ALU.mult)
                vtt(rp4, rp4, t13, ALU.add)
                vtt(t13, pk4, tbl4("ropeKs"), ALU.mult)
                vtt(k4, k4, tbl4("ropeKc"), ALU.mult)
                vtt(k4, k4, t13, ALU.add)
                n_chain = s_v.n
                rec_vt = {0: None, 1: None}
                for wl in range(4):
                    w0 = ()
                    if wl == 0:
                        w0 = [(s_v, n_qkv_v), (s_s, n_qkv_s)]
                    elif rec_vt[wl % 2]:
                        w0 = [rec_vt[wl % 2]]
                    for ch in range(NCH):
                        tr(psB[wl % 2][:, ch * 128:(ch + 1) * 128],
                           vraw[:, ch, wl * 128:(wl + 1) * 128],
                           csb["identbf"][:],
                           wait=w0 if ch == 0 else (),
                           inc=(s_t,) if ch == 7 else ())
                    vcp(vtm[:, wl, :, :],
                        psB[wl % 2][:].rearrange("p (a b) -> p a b", a=8),
                        wait=[(s_t, s_t.n)])
                    rec_vt[wl % 2] = (s_v, s_v.n)
                whgs = [(wl, hg) for wl in range(4) for hg in range(2)]
                sim_done.clear()
                pbf_done.clear()
                pbf_done[-1] = pbf_done[-2] = (s_v, 0)
                attT_done.clear()

                def emit_fill_sims(i):
                    wl, hg = whgs[i]
                    w = ss * 4 + wl
                    wc = slice(wl * 128, (wl + 1) * 128)
                    mk = "cmw0" if w == 0 else "cm2"
                    for b in range(4):
                        wf_ = [exp_rec[b]]
                        if i == 0:
                            wf_.append((s_s, n_rope_s))
                        mm(psF[b][:], csb["identbf"][:], csb[mk][:],
                           start=True, stop=True, wait=wf_)
                    for sl in range(8):
                        hh = SLOT8[sl]
                        h = hg * 8 + hh
                        ch = h // 2
                        base = (h % 2) * 64
                        bk = psF[sl // 2]
                        off = (sl % 2) * 256
                        lhs = qk[base:base + 64, ch, wc]
                        if i == 0 and sl == 0:
                            twait([(s_v, n_chain)])
                        if w > 0:
                            rhsP = (kPprev[base:base + 64, ch, :]
                                    if wl == 0 else
                                    rkP[base:base + 64, ch,
                                        (wl - 1) * 128:wl * 128])
                            mm(bk[:, off:off + 128], lhs, rhsP,
                               start=False, stop=True)
                        mm(bk[:, off + 128:off + 256], lhs,
                           qk[base:base + 64, 8 + ch, wc],
                           start=False, stop=True,
                           inc=(s_t,) if sl == 7 else ())
                    sim_done[i] = (s_t, s_t.n)

                def emit_trans(i):
                    nonlocal rec_pbf
                    wl, hg = whgs[i]
                    w = ss * 4 + wl
                    twait([pbf_done[i]])
                    for jc in range(2):
                        if w == 0 and jc == 0:
                            continue
                        for sl in range(8):
                            tr(psB[jc][:, sl * 128:sl * 128 + 128],
                               pbf[:, sl, jc * 128:(jc + 1) * 128],
                               csb["identbf"][:],
                               inc=(s_t,) if sl == 7 else ())
                    rec_pbf = (s_t, s_t.n)

                def emit_exp(i):
                    pe = pexp2[i % 2]
                    act(pe[:],
                        psF6[:, 0:4, :].rearrange("p a (b c) -> p (a b) c",
                                                  c=256),
                        AF.Exp, wait=[sim_done[i], pbf_done[i - 2]],
                        inc=(s_s,))
                    for b in range(4):
                        exp_rec[b] = (s_s, s_s.n)

                def emit_attTcp(i):
                    wl, hg = whgs[i]
                    w = ss * 4 + wl
                    if w == 0:
                        vcp(attT[:, :, 128:256],
                            psB[1][:].rearrange("p (a b) -> p a b", a=8),
                            wait=[rec_pbf])
                    else:
                        vcp(attT[:].rearrange("p a (b c) -> p a b c", c=128),
                            psB2[:].rearrange("p a (b c) -> p b a c", c=128),
                            wait=[rec_pbf])
                    attT_done[i] = (s_v, s_v.n)

                def emit_sumpbf(i):
                    pe = pexp2[i % 2]
                    vred(ssum[:, 0:8], pe[:], ALU.add,
                         wait=[(s_s, exp_rec[3][1])])
                    vrecip(rr[:, 0:8], ssum[:, 0:8])
                    vtt(pbf[:], pe[:],
                        rr[:, 0:8].rearrange("p (a b) -> p a b", b=1
                                             ).broadcast_to((128, 8, 256)),
                        ALU.mult, wait=[rec_pbf])
                    pbf_done[i] = (s_v, s_v.n)

                def emit_avm(i):
                    wl, hg = whgs[i]
                    w = ss * 4 + wl
                    for sl in range(8):
                        hh = SLOT8[sl]
                        h = hg * 8 + hh
                        ch = h // 2
                        bk = psF[4 + sl // 4]
                        off = (sl % 4) * 128
                        w0 = []
                        if sl == 0:
                            w0 = [attT_done[i], rec_avbank[0]]
                        elif sl == 4:
                            w0 = [rec_avbank[1]]
                        if w > 0:
                            vprev = (vtmprev[:, ch, :] if wl == 0 else
                                     vtm[:, wl - 1, ch, :])
                            mm(bk[:, off:off + 128], vprev,
                               attT[:, sl, 0:128], start=True,
                               stop=False, wait=w0)
                            w0 = []
                        mm(bk[:, off:off + 128], vtm[:, wl, ch, :],
                           attT[:, sl, 128:256], start=(w == 0),
                           stop=True, wait=w0,
                           inc=(s_t,) if sl == 7 else ())
                    av_done[i] = (s_t, s_t.n)

                def emit_ave(i):
                    wl, hg = whgs[i]
                    wc = slice(wl * 128, (wl + 1) * 128)
                    for half in range(2):
                        b0 = half * 64
                        act(aout[b0:b0 + 64, hg * 4:hg * 4 + 4, wc],
                            psF[4 + half][b0:b0 + 64, :].rearrange(
                                "p (a b) -> p a b", a=4), AF.Copy,
                            wait=[av_done[i]] if half == 0 else (),
                            inc=(s_s,))
                        rec_avbank[half] = (s_s, s_s.n)

                for i in range(9):
                    if i < 8:
                        emit_fill_sims(i)
                    if i >= 1:
                        emit_trans(i - 1)
                    if i < 8:
                        emit_exp(i)
                    if i >= 1:
                        emit_attTcp(i - 1)
                    if i < 8:
                        emit_sumpbf(i)
                    if i >= 1:
                        emit_avm(i - 1)
                        emit_ave(i - 1)
                vcp(kPprev[:], rkP[:, :, 384:512], wait=[(s_t, s_t.n)])
                vcp(vtmprev[:], vtm[:, 3, :, :])
                rec_op = {0: None, 1: None}
                for oc in range(NCH):
                    bk = psF[oc % 2]
                    for c in range(NCH):
                        w0 = ()
                        if c == 0:
                            w0 = ([(s_v, s_v.n), (s_s, s_s.n)]
                                  if oc == 0 else
                                  ([rec_op[oc % 2]] if rec_op[oc % 2]
                                   else []))
                        mm(bk[:], wout[:, c, oc * 128:(oc + 1) * 128],
                           aout[:, c, :], start=(c == 0), stop=(c == 7),
                           wait=w0, inc=(s_t,) if c == 7 else ())
                    vtt(x32[:, oc, scs], x32[:, oc, scs], bk[:], ALU.add,
                        wait=[(s_t, s_t.n)])
                    rec_op[oc % 2] = (s_v, s_v.n)
                if ss == 0:
                    emit_cache([k for k in ("l1_h2", "l1_w3")
                                if k not in wc_gate])
                em.barrier(exclude=em.barrier_exclude)
            emit_ln(sq_scr, outbf=True)

        def emit_l1f():
            ar.reset()
            wsl_mark = [ar.mark()]
            wsl = []
            for _ in range(2):
                wsl_mark.append(ar.mark())
                wsl.append(ar.take((128, 32, 256), BF16))
            wslA = [ar.view(wsl_mark[1], (128, 16, 512), BF16),
                    ar.view(wsl_mark[2], (128, 16, 512), BF16)]
            h_mark = ar.mark()
            h_sb = ar.take((128, 32, 512), BF16)
            g_mark = ar.mark()
            g_sb = ar.take((128, 32, 512), BF16)
            s1, s2 = smalls["l1_s1"], smalls["l1_s2"]
            sh1, sh2, s3 = (smalls["l1_sh1"], smalls["l1_sh2"],
                            smalls["l1_s3"])
            silt = ar.view(g_mark, (128, 2, 512), F32)
            sq_scr = ar.view(h_mark, (128, 8, 512), BF16)
            otm = [ar.view(g_mark + 4096 + 2048 * k, (128, 512), F32)
                   for k in range(4)]
            emit_cache([k for k in ("l1_w1", "l1_w2", "l1_h1", "l1_h2",
                                    "l1_w3") if k not in wc_gate])
            em.barrier(exclude=em.barrier_exclude)

            nD_v = s_v.n
            pend = None
            pending_store = None

            def emit_store(st):
                for tb in range(4):
                    trow = st * 512 + tb * 128
                    for cg in range(2):
                        bank = psF[4 + cg]
                        ob = otm[(tb % 2) * 2 + cg]
                        for j in range(4):
                            tr(bank[:, j * 128:(j + 1) * 128],
                               x32[:, cg * 4 + j, trow:trow + 128],
                               csb["ident32"][:],
                               wait=[(s_st, max(0, s_st.n - 32)),
                                     (s_v, s_v.n)] if j == 0 else (),
                               inc=(s_t,) if j == 3 else ())
                        vcp(ob, bank[:], wait=[(s_t, s_t.n)])
                        SY(lambda e, o=xout[trow:trow + 128,
                                            cg * 512:cg * 512 + 512],
                           i=ob: e.dma_start(out=o, in_=i),
                           wait=[(s_v, s_v.n)], inc=(s_st,))
                ln_rec["m"] = (s_v, s_v.n)
                ln_rec["v"] = (s_v, s_v.n)

            for tt in range(4):
                tc = slice(tt * 512, tt * 512 + 512)

                def slabA(gg, gate=None):
                    w = [(s_t, s_t.n)]
                    if gate is not None:
                        w = w + [(s_wc, gate)]
                    SYW2(lambda e, d=wslA[gg % 2][:, 0:8, :], o0=gg * 512:
                        e.dma_start(
                        out=d, in_=wbf["l1_w1"][:, o0:o0 + 512].rearrange(
                            "(c p) o -> p c o", p=128)), wait=w)
                    SCD(lambda e, d=wslA[gg % 2][:, 8:16, :], o0=gg * 512:
                        e.dma_start(
                        out=d, in_=wbf["l1_w2"][:, o0:o0 + 512].rearrange(
                            "(c p) o -> p c o", p=128)), wait=w)

                def slabH(gg, wn, gate=None):
                    w = [(s_t, s_t.n)]
                    if gate is not None:
                        w = w + [(s_wc, gate)]
                    SYW2(lambda e, d=wsl[gg % 2][:, 0:16, :], o0=gg * 256:
                        e.dma_start(out=d, in_=wbf[wn][
                            0:2048, o0:o0 + 256].rearrange(
                            "(c p) o -> p c o", p=128)), wait=w)
                    SCD(lambda e, d=wsl[gg % 2][:, 16:32, :], o0=gg * 256:
                        e.dma_start(out=d, in_=wbf[wn][
                            2048:4096, o0:o0 + 256].rearrange(
                            "(c p) o -> p c o", p=128)), wait=w)

                def slabD(gg, gate=None):
                    slabH(gg, "l1_w3", gate)

                gt = (lambda k: wc_gate[k] if tt == 0 else None)
                stage_w = [(s_v, nD_v)]
                if pend is None:
                    base_wa, base_wb = s_wa.n, s_wb.n
                    slabA(0, (max(wc_gate["l1_w1"], wc_gate["l1_w2"])
                              if tt == 0 else None))
                else:
                    base_wa, base_wb = pend
                    pend = None
                rec_silu = {0: (s_s, 0), 1: (s_s, 0)}
                rec_stt = {0: (s_v, 0), 1: (s_v, 0)}
                for g in range(8):
                    if g == 1 and pending_store is not None:
                        emit_store(pending_store)
                        pending_store = None
                    if g + 1 < 8:
                        slabA(g + 1)
                    else:
                        pend = (s_wa.n, s_wb.n)
                        slabH(0, "l1_h1", gt("l1_h1"))
                    w1v = wslA[g % 2]
                    for j in range(4):
                        oc = g * 4 + j
                        b = j % 2
                        bk1 = psF[b]
                        bk2 = psF[2 + b]
                        for c in range(NCH):
                            w0 = ()
                            if c == 0:
                                w0 = [(s_wa, base_wa + 16 * (g + 1)),
                                      (s_wb, base_wb + 16 * (g + 1)),
                                      rec_silu[b]]
                                if g == 0 and j == 0:
                                    w0 += stage_w
                            mm(bk1[:], w1v[:, c, j * 128:(j + 1) * 128],
                               xbf[:, c, tc], start=(c == 0), stop=(c == 7),
                               wait=w0)
                        for c in range(NCH):
                            mm(bk2[:], w1v[:, 8 + c, j * 128:(j + 1) * 128],
                               xbf[:, c, tc], start=(c == 0), stop=(c == 7),
                               wait=[rec_stt[b]] if c == 0 else (),
                               inc=(s_t,) if c == 7 else ())
                        act(silt[:, b, :], bk1[:], AF.Silu,
                            scale=s1[:, oc:oc + 1],
                            wait=[(s_t, s_t.n), rec_stt[b]], inc=(s_s,))
                        rec_silu[b] = (s_s, s_s.n)
                        vstt(h_sb[:, oc, :], bk2[:], s2[:, oc:oc + 1],
                             silt[:, b, :], ALU.mult, ALU.mult,
                             wait=[rec_silu[b]])
                        rec_stt[b] = (s_v, s_v.n)
                stage_w = [(s_v, s_v.n), (s_s, s_s.n), (s_st, s_st.n)]
                for wname, ssc, src_sb, dst_sb in (
                        ("l1_h1", sh1, h_sb, g_sb), ("l1_h2", sh2, g_sb,
                                                     h_sb)):
                    if pend is None:
                        base_wa, base_wb = s_wa.n, s_wb.n
                        slabH(0, wname, gt(wname))
                    else:
                        base_wa, base_wb = pend
                        pend = None
                    rec_gelu = {0: (s_s, 0), 1: (s_s, 0)}
                    for g in range(16):
                        if g + 1 < 16:
                            slabH(g + 1, wname)
                        else:
                            pend = (s_wa.n, s_wb.n)
                            if wname == "l1_h1":
                                slabH(0, "l1_h2", gt("l1_h2"))
                            else:
                                slabD(0, gt("l1_w3"))
                        for j in range(2):
                            oc = g * 2 + j
                            bk = psF[j]
                            for c in range(32):
                                w0 = ()
                                if c == 0:
                                    w0 = [(s_wa, base_wa + 16 * (g + 1)),
                                          (s_wb, base_wb + 16 * (g + 1)),
                                          rec_gelu[j]]
                                    if g == 0 and j == 0:
                                        w0 += stage_w
                                mm(bk[:], wsl[g % 2][:, c,
                                                     j * 128:(j + 1) * 128],
                                   src_sb[:, c, :], start=(c == 0),
                                   stop=(c == 31), wait=w0,
                                   inc=(s_t,) if c == 31 else ())
                            act(dst_sb[:, oc, :], bk[:], AF.Gelu,
                                scale=ssc[:, oc:oc + 1], wait=[(s_t, s_t.n)])
                            rec_gelu[j] = (s_s, s_s.n)
                    stage_w = [(s_s, s_s.n)]
                if pend is None:
                    base_wa, base_wb = s_wa.n, s_wb.n
                    slabD(0, gt("l1_w3"))
                else:
                    base_wa, base_wb = pend
                    pend = None
                rec_d = {0: (s_v, 0), 1: (s_v, 0)}
                for g in range(4):
                    if g + 1 < 4:
                        slabD(g + 1)
                    elif tt + 1 < 4:
                        pend = (s_wa.n, s_wb.n)
                        slabA(0)
                    for j in range(2):
                        oc = g * 2 + j
                        bk = psF[j]
                        for c in range(32):
                            w0 = ()
                            if c == 0:
                                w0 = [(s_wa, base_wa + 16 * (g + 1)),
                                      (s_wb, base_wb + 16 * (g + 1)),
                                      rec_d[j]]
                                if g == 0 and j == 0:
                                    w0 += stage_w
                            mm(bk[:], wsl[g % 2][:, c, j * 128:(j + 1) * 128],
                               h_sb[:, c, :], start=(c == 0), stop=(c == 31),
                               wait=w0, inc=(s_t,) if c == 31 else ())
                        vstt(x32[:, oc, tc], bk[:], s3[:, oc:oc + 1],
                             x32[:, oc, tc], ALU.mult, ALU.add,
                             wait=[(s_t, s_t.n)])
                        rec_d[j] = (s_v, s_v.n)
                nD_v = s_v.n
                emit_ln_tile(tt, sq_scr, outbf=False)
                if out_tok:
                    if tt + 1 < 4:
                        pending_store = tt
                    else:
                        emit_store(tt)
            em.barrier(exclude=em.barrier_exclude)

        for ph in phases:
            dict(l0a=emit_l0a, l0f=emit_l0f, l1a=emit_l1a,
                 l1f=emit_l1f)[ph]()

        if not out_tok:
            em.barrier(exclude=em.barrier_exclude)
            SY(lambda e: e.dma_start(
                out=xout.rearrange("(c p) t -> p c t", p=128), in_=x32[:]),
                inc=(s_st,))
        em.op("sync", None, [(s_st, s_st.n)], [])

        for s in em.sems:
            s.h = E(nc.semaphore(s.name))
        block = E(nc.Block())

        def make_body(items):
            def body(e):
                for _, fn, waits, incs in items:
                    for (sm, val) in waits:
                        if val > 0:
                            e.wait_ge(sm.h, val)
                    ins = fn(e) if fn is not None else None
                    if incs:
                        if ins is None:
                            ins = e.nop()
                        for (sm, k) in incs:
                            ins.then_inc(sm.h, k)
            return body

        for eng in ENGINES:
            items = [it for it in em.items if it[0] == eng]
            if items:
                getattr(block, eng)(make_body(items))

    return nc, names


_PROG_CACHE = {}


def _get_prog(phases):
    key = tuple(phases)
    if key not in _PROG_CACHE:
        _PROG_CACHE[key] = build_program(list(phases))
    return _PROG_CACHE[key]


LAUNCH_GROUPS = [("l0a", "l0f", "l1a", "l1f")]


def kernel(**inputs):
    w = host_weights(inputs)
    c = host_consts()
    x = _f32(inputs["L0_attn_x"])
    cur = [np.ascontiguousarray(x[b]) for b in range(NC)]

    for phases in LAUNCH_GROUPS:
        nc, names = _get_prog(phases)
        in_maps = []
        for b in range(NC):
            m = {}
            for nm in names:
                if nm in ("xin", "xin_fm"):
                    m[nm] = cur[b]
                elif nm in c:
                    m[nm] = c[nm]
                else:
                    m[nm] = w[nm]
            in_maps.append(m)
        res = run_bass_kernel_spmd(nc, in_maps, core_ids=list(range(NC)))
        okey = "xout" if phases[-1] == "l1f" else "xout_fm"
        cur = [np.ascontiguousarray(np.asarray(res.results[b][okey]))
               for b in range(NC)]

    return np.stack(cur, axis=0).astype(np.float32)


# version 30
# speedup vs baseline: 1.0564x; 1.0100x over previous
import contextlib
import numpy as np
import ml_dtypes
import concourse.bass as bass
import concourse.mybir as mybir
from concourse.bass_utils import run_bass_kernel_spmd
from concourse.hw_specs import get_activation_tables

F32 = mybir.dt.float32
BF16 = mybir.dt.bfloat16
AF = mybir.ActivationFunctionType
ALU = mybir.AluOpType
AX = mybir.AxisListType

NC = 8
T = 2048
D = 1024
NCH = 8
EPS = 1e-5
INVD = 1.0 / D
NEG = -3.0e38

L0_DILS = [1, 2, 4, 8]
L0_NB = 16
L0_NORM = float(1.0 / np.sqrt(128.0))

_SET_IDX = None
_FUNC_ANY = {AF.Copy, AF.Identity, AF.Relu, AF.Square, AF.Sign, AF.Abs}
_FUNC_SET = {
    AF.Exp: "natural_log_exp_and_others",
    AF.Ln: "natural_log_exp_and_others",
    AF.Sigmoid: "sigmoid_and_others",
    AF.Gelu: "gelu_and_others",
    AF.Silu: "silu_and_others",
}


def _act_set_idx(nc, name):
    global _SET_IDX
    if _SET_IDX is None:
        _SET_IDX = {n: i for i, n in
                    enumerate(get_activation_tables(nc.m.arch).keys())}
    return _SET_IDX[name]


def _f32(x):
    return np.ascontiguousarray(np.asarray(x), dtype=np.float32)


def _pc(v, nch):
    return np.ascontiguousarray(_f32(v).reshape(nch, 128).T)


def host_weights(i):
    w = {}
    w["l0_wqkv"] = np.ascontiguousarray(np.stack(
        [_f32(i["L0_attn_Wk"]), _f32(i["L0_attn_Wq"]), _f32(i["L0_attn_Wv"])],
        axis=0).transpose(0, 1, 3, 2))
    w["l0_wout"] = np.ascontiguousarray(_f32(i["L0_attn_W_out"]).T)
    w["l0_bout"] = _pc(i["L0_attn_b_out"], 8)
    w["l0f_w"] = np.ascontiguousarray(np.stack(
        [_f32(i["L0_ffn_W_r"]).T, _f32(i["L0_ffn_W_k"]).T,
         _f32(i["L0_ffn_W_v"]).T], axis=0))
    w["l0f_mur"] = _pc(i["L0_ffn_mu_r"], 8)
    w["l0f_muk"] = _pc(i["L0_ffn_mu_k"], 8)
    w["l1_wqkv"] = _f32(i["L1_attn_W_qkv"])
    w["l1_wout"] = _f32(i["L1_attn_W_out"])
    w["l1_w1"] = np.ascontiguousarray(_f32(i["L1_ffn_w1_w"]).T)
    w["l1_w2"] = np.ascontiguousarray(_f32(i["L1_ffn_w2_w"]).T)
    w["l1_h1"] = np.ascontiguousarray(_f32(i["L1_ffn_h1_w"]).T)
    w["l1_h2"] = np.ascontiguousarray(_f32(i["L1_ffn_h2_w"]).T)
    w["l1_w3"] = np.ascontiguousarray(_f32(i["L1_ffn_w3_w"]).T)
    w["l1_s1"] = _pc(i["L1_ffn_w1_s"], 32)
    w["l1_s2"] = _pc(i["L1_ffn_w2_s"], 32)
    w["l1_sh1"] = _pc(i["L1_ffn_h1_s"], 32)
    w["l1_sh2"] = _pc(i["L1_ffn_h2_s"], 32)
    w["l1_s3"] = _pc(i["L1_ffn_w3_s"], 8)
    return w


def host_consts():
    c = {}
    c["ident32"] = np.eye(128, dtype=np.float32)
    c["identbf"] = np.eye(128, dtype=ml_dtypes.bfloat16)
    c["onesbf"] = np.ones((128, 128), dtype=ml_dtypes.bfloat16)
    c["zerobf"] = np.zeros((128, 128), dtype=ml_dtypes.bfloat16)
    dh = 64
    inv = 1.0 / (10000.0 ** (np.arange(0, dh, 2, dtype=np.float64) / dh))
    pos = np.arange(256, dtype=np.float64)
    fr = pos[:, None] * inv[None, :]
    fr = np.concatenate([fr, fr], axis=1)
    sc = 1.0 / np.sqrt(dh)
    dup = lambda t: np.ascontiguousarray(
        np.concatenate([t, t], axis=0), dtype=ml_dtypes.bfloat16)
    c["ropeQc"] = dup(np.cos(fr[128:]).T * sc)
    c["ropeQs"] = dup(np.sin(fr[128:]).T * sc)
    c["ropeKc"] = dup(np.cos(fr[128:]).T)
    c["ropeKs"] = dup(np.sin(fr[128:]).T)
    c["ropePc"] = dup(np.cos(fr[:128]).T)
    c["ropePs"] = dup(np.sin(fr[:128]).T)
    P = np.zeros((128, 128), dtype=np.float32)
    for hb in range(2):
        b = hb * 64
        for j in range(32):
            P[b + j, b + 32 + j] = -1.0
            P[b + 32 + j, b + j] = 1.0
    c["pmat"] = np.ascontiguousarray(P.T.astype(ml_dtypes.bfloat16))
    ii = np.arange(128)[:, None]
    jj = np.arange(128)[None, :]
    cm = np.full((128, 256), NEG, dtype=np.float32)
    cm[:, 0:128][jj >= ii] = 0.0
    cm[:, 128:256][jj <= ii] = 0.0
    c["cm2"] = np.ascontiguousarray(
        np.concatenate([cm, cm], axis=1).astype(ml_dtypes.bfloat16))
    cw = np.full((128, 256), NEG, dtype=np.float32)
    cw[:, 128:256][jj <= ii] = 0.0
    c["cmw0"] = np.ascontiguousarray(
        np.concatenate([cw, cw], axis=1).astype(ml_dtypes.bfloat16))
    return c


ENGINES = ("sync", "gpsimd", "tensor", "vector", "scalar")


class Sem:
    __slots__ = ("name", "n", "h")

    def __init__(self, name):
        self.name = name
        self.n = 0
        self.h = None


class Em:
    def __init__(self):
        self.items = []
        self.sems = []
        self.cur_set = None
        self.auto_sems = {}
        self.barrier_exclude = ()

    def sem(self, name):
        s = Sem(name)
        self.sems.append(s)
        return s

    def op(self, eng, fn=None, wait=(), inc=(), dma=False):
        w = [(x, x.n) if isinstance(x, Sem) else x for x in wait]
        if fn is not None and not dma and eng in self.auto_sems:
            a = self.auto_sems[eng]
            w.append((a, a.n))
        ii = []
        for x in inc:
            k = 16 if dma else 1
            if isinstance(x, tuple):
                x, k = x
            x.n += k
            ii.append((x, k))
        self.items.append((eng, fn, w, ii))

    def barrier(self, exclude=(), engines=ENGINES):
        waits = [(s, s.n) for s in self.sems if s.n > 0 and s not in exclude]
        for eng in engines:
            self.items.append((eng, None, list(waits), []))


class Arena:
    def __init__(self, ap8, nbytes):
        self.ap8 = ap8
        self.nbytes = nbytes
        self.off = 0
        self.hwm = 0

    def reset(self):
        self.off = 0

    def view(self, off, shape, dtype):
        esz = 2 if dtype == BF16 else 4
        n = int(np.prod(shape[1:])) * esz
        assert off + n <= self.nbytes
        v = self.ap8[:, off:off + n].bitcast(dtype)
        if len(shape) == 2:
            return v
        dims = "abcdef"[:len(shape) - 1]
        pat = "p (" + " ".join(dims) + ") -> p " + " ".join(dims)
        kw = {d: shape[1 + i] for i, d in enumerate(dims)}
        return v.rearrange(pat, **kw)

    def mark(self):
        return (self.off + 63) & ~63

    def take(self, shape, dtype):
        esz = 2 if dtype == BF16 else 4
        n = int(np.prod(shape[1:])) * esz
        off = (self.off + 63) & ~63
        assert off + n <= self.nbytes, f"arena overflow {off}+{n}>{self.nbytes}"
        self.off = off + n
        self.hwm = max(self.hwm, self.off)
        return self.view(off, shape, dtype)


def build_program(phases):
    first, last = phases[0], phases[-1]
    in_tok = first == "l0a"
    out_tok = last == "l1f"

    nc = bass.Bass("TRN2", target_bir_lowering=False)
    em = Em()
    names = []

    def din(name, shape, dtype=F32):
        names.append(name)
        return nc.dram_tensor(name, list(shape), dtype,
                              kind="ExternalInput").ap()

    import os
    dbg_cat = os.environ.get("K_DBG_CAT") == "1" and phases == ["l0a"]
    dbg_pre = os.environ.get("K_DBG_PRE") == "1" and phases == ["l0a"]
    dbg_att = os.environ.get("K_DBG_ATT") == "1" and phases == ["l0a"]
    l1a_lvl = int(os.environ.get("K_L1A_LVL", "3"))
    xin = din("xin" if in_tok else "xin_fm", (T, D) if in_tok else (D, T))
    dbg_out = (nc.dram_tensor("dbg_cat", [512, T], BF16,
                              kind="ExternalOutput").ap() if dbg_cat else None)
    dbg_pre_out = (nc.dram_tensor("dbg_pre", [D, T], F32,
                                  kind="ExternalOutput").ap() if dbg_pre
                   else None)
    dbg_att_out = (nc.dram_tensor("dbg_att", [4, 128, 16, 128], F32,
                                  kind="ExternalOutput").ap() if dbg_att
                   else None)
    dbg_smd_out = (nc.dram_tensor("dbg_smd", [4, 128, 256], F32,
                                  kind="ExternalOutput").ap() if dbg_att
                   else None)
    dbg_sm2_out = (nc.dram_tensor("dbg_sm2", [4, 128, 576], F32,
                                  kind="ExternalOutput").ap() if dbg_att
                   else None)
    dbg_kqv_out = (nc.dram_tensor("dbg_kqv", [4, 128, 16, 384], F32,
                                  kind="ExternalOutput").ap() if dbg_att
                   else None)
    xout = nc.dram_tensor("xout" if out_tok else "xout_fm",
                          [T, D] if out_tok else [D, T], F32,
                          kind="ExternalOutput").ap()

    cshapes = dict(ident32=((128, 128), F32), identbf=((128, 128), BF16),
                   onesbf=((128, 128), BF16), zerobf=((128, 128), BF16),
                   ropeQc=((128, 128), BF16), ropeQs=((128, 128), BF16),
                   ropeKc=((128, 128), BF16), ropeKs=((128, 128), BF16),
                   ropePc=((128, 128), BF16), ropePs=((128, 128), BF16),
                   pmat=((128, 128), BF16), cm2=((128, 512), BF16),
                   cmw0=((128, 512), BF16))
    need_c = {"ident32", "identbf", "onesbf"}
    if "l0a" in phases:
        need_c |= {"zerobf"}
    if "l1a" in phases:
        need_c |= {"ropeQc", "ropeQs", "ropeKc", "ropeKs", "ropePc", "ropePs",
                   "pmat", "cm2", "cmw0"}
    wshapes = dict(l0_wqkv=(3, 4, 1024, 128), l0_wout=(512, 1024),
                   l0_bout=(128, 8), l0f_w=(3, 1024, 1024),
                   l0f_mur=(128, 8), l0f_muk=(128, 8),
                   l1_wqkv=(1024, 3072), l1_wout=(1024, 1024),
                   l1_w1=(1024, 4096), l1_w2=(1024, 4096),
                   l1_h1=(4096, 4096), l1_h2=(4096, 4096), l1_w3=(4096, 1024),
                   l1_s1=(128, 32), l1_s2=(128, 32), l1_sh1=(128, 32),
                   l1_sh2=(128, 32), l1_s3=(128, 8))
    need_w = set()
    if "l0a" in phases:
        need_w |= {"l0_wqkv", "l0_wout", "l0_bout"}
    if "l0f" in phases:
        need_w |= {"l0f_w", "l0f_mur", "l0f_muk"}
    if "l1a" in phases:
        need_w |= {"l1_wqkv", "l1_wout"}
    if "l1f" in phases:
        need_w |= {"l1_w1", "l1_w2", "l1_h1", "l1_h2", "l1_w3",
                   "l1_s1", "l1_s2", "l1_sh1", "l1_sh2", "l1_s3"}
    cst = {k: din(k, *cshapes[k]) for k in sorted(need_c)}
    wdr = {k: din(k, wshapes[k]) for k in sorted(need_w)}
    wbf = {}
    if "l1f" in phases:
        for k in ("l1_w1", "l1_w2", "l1_h1", "l1_h2", "l1_w3"):
            wbf[k] = nc.dram_tensor("bfc_" + k, list(wshapes[k]), BF16,
                                    kind="Internal").ap()
    if "l1a" in phases:
        wbf["l1_wqkv"] = nc.dram_tensor("bfc_l1_wqkv",
                                        list(wshapes["l1_wqkv"]), BF16,
                                        kind="Internal").ap()

    with contextlib.ExitStack() as ctx:
        E = ctx.enter_context
        x32 = E(nc.sbuf_tensor("x32", [128, NCH, T], F32))
        xbf = E(nc.sbuf_tensor("xbf", [128, NCH, T], BF16))
        csb = {k: E(nc.sbuf_tensor("c_" + k, list(cshapes[k][0]),
                                   cshapes[k][1])) for k in sorted(need_c)}
        smalls = {k: E(nc.sbuf_tensor("w_" + k, list(wshapes[k]), F32))
                  for k in sorted(need_w) if wshapes[k][0] == 128}
        ln_m = E(nc.sbuf_tensor("ln_m", [128, 512], F32))
        epsb = E(nc.sbuf_tensor("epsb", [128, 1], F32))
        junk = E(nc.sbuf_tensor("junk", [128, 8], F32))
        ln_r = E(nc.sbuf_tensor("ln_r", [128, 512], F32))
        ln_t = E(nc.sbuf_tensor("ln_t", [128, 512], F32))
        AB = 98 * 1024
        ar8 = E(nc.sbuf_tensor("arena", [128, AB], mybir.dt.uint8))
        ar = Arena(ar8[:], AB)
        psF6 = E(nc.psum_tensor("psF6", [128, 6, 512], F32))
        psB2 = E(nc.psum_tensor("psB2", [128, 2, 1024], BF16))
        psF = [psF6[:, i, :] for i in range(6)]
        psB = [psB2[:, i, :] for i in range(2)]

        s_ld = em.sem("s_ld")
        s_ti = [em.sem(f"s_ti{k}") for k in range(4)]
        s_wl = em.sem("s_wl")
        s_t = em.sem("s_t")
        s_v = em.sem("s_v")
        s_s = em.sem("s_s")
        s_g = em.sem("s_g")
        s_st = em.sem("s_st")
        s_wa = em.sem("s_wa")
        s_wb = em.sem("s_wb")
        s_wc = em.sem("s_wc")
        em.auto_sems = {"vector": s_v, "scalar": s_s, "gpsimd": s_g}
        em.barrier_exclude = (s_wc,)

        def SYW2(fn, wait=(), inc=(s_wa,)):
            em.op("sync", fn, wait, inc, dma=True)

        def SCD(fn, wait=(), inc=(s_wb,)):
            em.op("scalar", fn, wait, inc, dma=True)

        def SY(fn, wait=(), inc=(s_ld,)):
            em.op("sync", fn, wait, inc, dma=True)

        def GD(fn, wait=(), inc=(s_wl,)):
            em.op("gpsimd", fn, wait, inc, dma=True)

        def mm(out, lhsT, rhs, start, stop, wait=(), inc=()):
            em.op("tensor", lambda e: e.matmul(out, lhsT, rhs, start=start,
                                               stop=stop), wait, inc)

        def tr(out, in_, ident, wait=(), inc=()):
            em.op("tensor", lambda e: e.transpose(out, in_, ident), wait, inc)

        def twait(wait):
            em.op("tensor", None, wait, ())

        def vtt(out, in0, in1, alu, wait=(), inc=(s_v,), eng="vector"):
            em.op(eng, lambda e: e.tensor_tensor(out=out, in0=in0, in1=in1,
                                                 op=alu), wait, inc)

        def vts(out, in0, s1, alu, wait=(), inc=(s_v,), eng="vector"):
            em.op(eng, lambda e: e.tensor_single_scalar(out=out, in_=in0,
                                                        scalar=s1, op=alu),
                  wait, inc)

        def vstt(out, in0, scal, in1, op0, op1, wait=(), inc=(s_v,)):
            em.op("vector", lambda e: e.scalar_tensor_tensor(
                out=out, in0=in0, scalar=scal, in1=in1, op0=op0, op1=op1),
                wait, inc)

        def vcp(out, in_, wait=(), inc=(s_v,), eng="vector"):
            em.op(eng, lambda e: e.tensor_copy(out=out, in_=in_), wait, inc)

        def vred(out, in_, alu, wait=(), inc=(s_v,)):
            em.op("vector", lambda e: e.tensor_reduce(out=out, in_=in_,
                                                      axis=AX.X, op=alu),
                  wait, inc)

        def vrecip(out, in_, wait=(), inc=(s_v,)):
            em.op("vector", lambda e: e.reciprocal(out=out, in_=in_),
                  wait, inc)

        def vmset(ap, val, wait=(), inc=(s_v,), eng="vector"):
            em.op(eng, lambda e: e.memset(ap, val), wait, inc)

        def vsettle(eng="vector"):
            sems = {"vector": s_v, "scalar": s_s, "gpsimd": s_g}
            col = {"vector": 0, "scalar": 1, "gpsimd": 2}[eng]
            if eng == "scalar":
                em.op(eng, lambda e: e.memzero(junk[:, col:col + 1]), (),
                      (sems[eng],))
            else:
                em.op(eng, lambda e, c=col: e.memset(junk[:, c:c + 1], 0.0),
                      (), (sems[eng],))

        def act(out, in_, func, bias=0.0, scale=1.0, accum_out=None,
                wait=(), inc=(s_s,)):
            if func not in _FUNC_ANY:
                sname = _FUNC_SET[func]
                if em.cur_set != sname:
                    sid = _act_set_idx(nc, sname)
                    em.op("scalar", lambda e, sid=sid: e.add_instruction(
                        mybir.InstLoadActFuncSet(
                            name=e.bass.get_next_instruction_name(),
                            act_func_set_id=sid, ins=[], outs=[])))
                    em.cur_set = sname
            em.op("scalar", lambda e: e.activation(
                out=out, in_=in_, func=func, bias=bias, scale=scale,
                accum_out=accum_out), wait, inc)

        for k in sorted(need_c):
            SY(lambda e, d=csb[k], s=cst[k]: e.dma_start(out=d[:], in_=s))
        for k in sorted(smalls):
            SY(lambda e, d=smalls[k], s=wdr[k]: e.dma_start(out=d[:], in_=s))

        vmset_done = em.op("vector", lambda e: e.memset(epsb[:], EPS), (), ((s_v, 1),))
        if in_tok:
            tin = ar.view(34 * 1024, (128, 16, 1024), F32)
            xin_r = xin.rearrange("(tb p) c -> p tb c", p=128)
            for k in range(4):
                SY(lambda e, d=tin[:, 4 * k:4 * k + 4, :],
                   s=xin_r[:, 4 * k:4 * k + 4, :]:
                   e.dma_start(out=d, in_=s), inc=(s_ti[k],))
            rec_in = {0: None, 1: None}
            for tb in range(16):
                for cg in range(2):
                    bank = psF[cg % 2]
                    for j in range(4):
                        w0 = ()
                        if j == 0:
                            w0 = [(s_ti[tb // 4], 16)]
                            if tb == 0 and cg == 0:
                                w0 = w0 + [(s_ld, s_ld.n)]
                            if rec_in[cg % 2]:
                                w0 = w0 + [rec_in[cg % 2]]
                        tr(bank[:, j * 128:(j + 1) * 128],
                           tin[:, tb, (cg * 4 + j) * 128:
                               (cg * 4 + j) * 128 + 128],
                           csb["ident32"][:],
                           wait=w0,
                           inc=(s_t,) if j == 3 else ())
                    vcp(x32[:, cg * 4:cg * 4 + 4, tb * 128:(tb + 1) * 128],
                        bank[:].rearrange("p (a b) -> p a b", a=4),
                        wait=[(s_t, s_t.n)])
                    vcp(xbf[:, cg * 4:cg * 4 + 4, tb * 128:(tb + 1) * 128],
                        bank[:].rearrange("p (a b) -> p a b", a=4))
                    rec_in[cg % 2] = (s_v, s_v.n)
            ar.reset()
        else:
            SY(lambda e: e.dma_start(
                out=x32[:], in_=xin.rearrange("(c p) t -> p c t", p=128)))
            GD(lambda e: e.dma_start(
                out=xbf[:], in_=xin.rearrange("(c p) t -> p c t", p=128)))
        wc_gate = {}

        def emit_cache(keys):
            keys = [k for k in keys if k in wbf]
            for k in keys:
                kk = wdr[k].shape[0]
                for half in range(2 if kk >= 4096 else 1):
                    hh = kk // 2 if kk >= 4096 else kk
                    GD(lambda e, d=wbf[k][half * hh:(half + 1) * hh, :],
                        src=wdr[k][half * hh:(half + 1) * hh, :]:
                        e.dma_start(out=d, in_=src), inc=(s_wc,))
                wc_gate[k] = s_wc.n
        em.barrier(exclude=em.barrier_exclude)

        def emit_ln(sq_scr, outbf):
            base = [(s_v, s_v.n), (s_s, s_s.n)]
            for tt in range(4):
                emit_ln_tile(tt, sq_scr, outbf, base=base)

        ln_rec = {"m": (s_v, 0), "v": (s_v, 0), "ap": (s_v, 0),
                  "sq": (s_t, 0)}

        def emit_ln_tile(tt, sq_scr, outbf, base=None):
            tc = slice(tt * 512, tt * 512 + 512)
            ybf = xbf[:, :, tc]
            act(sq_scr, x32[:, :, tc], AF.Square,
                wait=([base[0], ln_rec["sq"]] if base
                      else [(s_v, s_v.n)]), inc=())
            vsettle("scalar")
            vcp(ybf, x32[:, :, tc], inc=())
            vsettle()
            twait([(s_v, s_v.n), (s_s, s_s.n)])
            for c in range(NCH):
                mm(psF[4][:], csb["onesbf"][:], ybf[:, c, :],
                   start=(c == 0), stop=(c == 7),
                   wait=[ln_rec["m"]] if c == 0 else (),
                   inc=(s_t,) if c == 7 else ())
            for c in range(NCH):
                mm(psF[5][:], csb["onesbf"][:], sq_scr[:, c, :],
                   start=(c == 0), stop=(c == 7),
                   wait=[ln_rec["v"]] if c == 0 else (),
                   inc=(s_t,) if c == 7 else ())
            ln_rec["sq"] = (s_t, s_t.n)
            vts(ln_m[:], psF[4][:], INVD, ALU.mult,
                wait=[(s_t, s_t.n), ln_rec["ap"]])
            ln_rec["m"] = (s_v, s_v.n)
            vtt(ln_t[:], ln_m[:], ln_m[:], ALU.mult)
            vstt(ln_t[:], psF[5][:], INVD, ln_t[:], ALU.mult, ALU.subtract)
            ln_rec["v"] = (s_v, s_v.n)
            act(ln_t[:], ln_t[:], AF.Ln, bias=epsb[:], wait=[(s_v, s_v.n)])
            act(ln_r[:], ln_t[:], AF.Exp, scale=-0.5, inc=(s_s,))
            mb = ln_m[:].rearrange("p (a b) -> p a b", a=1
                                   ).broadcast_to((128, NCH, 512))
            rb = ln_r[:].rearrange("p (a b) -> p a b", a=1
                                   ).broadcast_to((128, NCH, 512))
            x3d = x32[:, :, tc]
            vtt(x3d, x3d, mb, ALU.subtract, wait=[(s_s, s_s.n)])
            vtt(x3d, x3d, rb, ALU.mult)
            ln_rec["ap"] = (s_v, s_v.n)
            if outbf:
                vcp(xbf[:, :, tc], x3d)

        def emit_l0a():
            ar.reset()
            wqkv = ar.take((128, 3, 4, 8, 128), BF16)
            wout = ar.take((128, 4, 1024), BF16)
            kqv_mark = ar.mark()
            kqv = ar.take((128, 16, 384), BF16)
            tmpd = ar.take((128, 16, 128), BF16)
            kqd = ar.take((128, 16, 16), F32)
            smdbf = ar.take((128, 16, 16), BF16)
            mx = ar.take((128, 16), F32)
            nb = ar.take((128, 16), F32)
            ssum = ar.take((128, 16), F32)
            rr = ar.take((128, 16), F32)
            att = ar.take((128, 16, 128), F32)
            attbf = ar.take((128, 16, 128), BF16)
            cat = ar.take((128, 4, 2048), BF16)
            sq_scr = ar.view(kqv_mark, (128, 8, 512), BF16)
            GD(lambda e: e.dma_start(
                out=wqkv, in_=wdr["l0_wqkv"].rearrange(
                    "a b (c p) k -> p a b c k", p=128)))
            GD(lambda e: e.dma_start(
                out=wout, in_=wdr["l0_wout"].rearrange("(v p) o -> p v o",
                                                       p=128)))
            vmset(cat, 0.0)
            emit_cache(("l1_wqkv", "l1_w1", "l1_w2"))
            em.barrier(exclude=em.barrier_exclude)

            evac_hist = []
            rec_scatter = (s_v, 0)
            rec_exp = (s_s, 0)
            for h, dil in enumerate(L0_DILS):
                L = L0_NB // dil
                ngroups = (L + 3) // 4
                for g in range(ngroups):
                    gs = min(4, L - g * 4)
                    bks = [psF[(g % 2) * 3 + j] for j in range(3)]
                    for j in range(gs):
                        n = g * 4 + j
                        Bn = n * dil
                        for c in range(NCH):
                            for proj in range(3):
                                w0 = ()
                                if (c == 0 and j == 0 and proj == 0 and
                                        len(evac_hist) >= 2):
                                    w0 = [(s_v, evac_hist[-2])]
                                mm(bks[proj][:, j * 128:(j + 1) * 128],
                                   xbf[:, c, Bn * 128:(Bn + 1) * 128],
                                   wqkv[:, proj, h, c, :],
                                   start=(c == 0), stop=(c == 7),
                                   wait=w0,
                                   inc=(s_t,) if (c == 7 and proj == 2 and
                                                  j == gs - 1) else ())
                    wt_ = [(s_t, s_t.n)]
                    for proj in range(3):
                        vcp(kqv[:, g * 4:g * 4 + gs,
                                proj * 128:(proj + 1) * 128],
                            bks[proj][:, 0:gs * 128].rearrange(
                                "p (a b) -> p a b", a=gs),
                            wait=wt_ if proj == 0 else ())
                    evac_hist.append(s_v.n)
                vmset(kqd[:, 0:L, :], NEG, wait=[rec_exp])
                for n in range(L):
                    vtt(tmpd[:, 0:n + 1, :],
                        kqv[:, n:n + 1, 0:128].broadcast_to((128, n + 1, 128)),
                        kqv[:, 0:n + 1, 128:256], ALU.mult)
                    vred(kqd[:, n, 0:n + 1], tmpd[:, 0:n + 1, :], ALU.add)
                k3 = kqd[:, 0:L, :]
                vred(mx[:, 0:L], k3, ALU.max)
                vts(nb[:, 0:L], mx[:, 0:L], -1.0, ALU.mult)
                vtt(k3, k3, nb[:, 0:L].rearrange("p (a b) -> p a b", b=1
                                                 ).broadcast_to((128, L, 16)),
                    ALU.add)
                act(k3, k3, AF.Exp, scale=L0_NORM, wait=[(s_v, s_v.n)])
                rec_exp = (s_s, s_s.n)
                vred(ssum[:, 0:L], k3, ALU.add, wait=[rec_exp])
                vrecip(rr[:, 0:L], ssum[:, 0:L])
                vtt(smdbf[:, 0:L, :], k3,
                    rr[:, 0:L].rearrange("p (a b) -> p a b", b=1
                                         ).broadcast_to((128, L, 16)),
                    ALU.mult)
                vmset(att[:, 0:L, :], 0.0)
                for n in range(L):
                    vtt(tmpd[:, 0:n + 1, :],
                        kqv[:, n:n + 1, 256:384].broadcast_to(
                            (128, n + 1, 128)),
                        smdbf[:, n, 0:n + 1].rearrange(
                            "p (a b) -> p a b", b=1).broadcast_to(
                            (128, n + 1, 128)), ALU.mult)
                    vtt(att[:, 0:n + 1, :], att[:, 0:n + 1, :],
                        tmpd[:, 0:n + 1, :], ALU.add)
                vcp(attbf[:, 0:L, :], att[:, 0:L, :], inc=())
                vsettle()
                twait([(s_v, s_v.n)])
                for m in range(L):
                    tr(psB[(m // 8) % 2][:, (m % 8) * 128:(m % 8) * 128 + 128],
                       attbf[:, m, :], csb["identbf"][:],
                       wait=[rec_scatter] if m == 0 else (),
                       inc=(s_t,) if (m % 8 == 7 or m == L - 1) else ())
                wt_ = [(s_t, s_t.n)]
                catv = cat[:, h, :].rearrange("p (s x) -> p s x", x=16)
                for m in range(L):
                    vtt(catv[:, :, m * dil],
                        psB[(m // 8) % 2][:, (m % 8) * 128:(m % 8) * 128 + 128],
                        csb["zerobf"][:], ALU.add, wait=wt_ if m == 0 else ())
                rec_scatter = (s_v, s_v.n)
            em.barrier(exclude=em.barrier_exclude)
            rec_op = {0: None, 1: None}
            first_op = True
            for tt in range(4):
                tc = slice(tt * 512, tt * 512 + 512)
                for oc in range(NCH):
                    bk = psF[oc % 2]
                    for cv in range(4):
                        w0 = ()
                        if cv == 0:
                            w0 = ([(s_v, s_v.n)] if first_op else
                                  ([rec_op[oc % 2]] if rec_op[oc % 2]
                                   else []))
                            first_op = False
                        mm(bk[:], wout[:, cv, oc * 128:(oc + 1) * 128],
                           cat[:, cv, tc], start=(cv == 0), stop=(cv == 3),
                           wait=w0, inc=(s_t,) if cv == 3 else ())
                    vstt(x32[:, oc, tc], bk[:], smalls["l0_bout"][:, oc:oc + 1],
                         x32[:, oc, tc], ALU.add, ALU.add,
                         wait=[(s_t, s_t.n)])
                    rec_op[oc % 2] = (s_v, s_v.n)
            em.barrier(exclude=em.barrier_exclude)
            if dbg_cat:
                SY(lambda e: e.dma_start(
                    out=dbg_out.rearrange("(v p) t -> p v t", p=128),
                    in_=cat[:]),
                    wait=[(s_v, s_v.n), (s_s, s_s.n), (s_t, s_t.n)],
                    inc=(s_st,))
                em.barrier(exclude=em.barrier_exclude)
            if dbg_pre:
                SY(lambda e: e.dma_start(
                    out=dbg_pre_out.rearrange("(c p) t -> p c t", p=128),
                    in_=x32[:]),
                    wait=[(s_v, s_v.n), (s_s, s_s.n), (s_t, s_t.n)],
                    inc=(s_st,))
                em.barrier(exclude=em.barrier_exclude)
            emit_ln(sq_scr, outbf=False)

        def emit_l0f():
            ar.reset()
            wf = ar.take((128, 3, 8, 1024), BF16)
            d_mark = ar.mark()
            d32 = ar.take((128, 8, 512), BF16)
            xr2 = [ar.take((128, 8, 512), BF16) for _ in range(2)]
            xk2 = [ar.take((128, 8, 512), BF16) for _ in range(2)]
            rbf = ar.take((128, 8, 512), BF16)
            tprod = ar.take((128, 1, 512), F32)
            sq_scr = ar.view(d_mark, (128, 8, 512), BF16)
            mur = smalls["l0f_mur"]
            muk = smalls["l0f_muk"]
            GD(lambda e: e.dma_start(
                out=wf, in_=wdr["l0f_w"].rearrange("a (c p) o -> p a c o",
                                                   p=128)),
                wait=[(s_v, s_v.n), (s_s, s_s.n), (s_t, s_t.n)])
            emit_cache(("l1_h1",))
            em.barrier(exclude=em.barrier_exclude)
            TQ = 512
            rec_free = {0: (s_t, 0), 1: (s_t, 0)}
            rec_bank = {0: (s_v, 0), 1: (s_v, 0)}
            prep_done = {}

            def prep(q):
                t0 = q * TQ
                xr = xr2[q % 2]
                xk = xk2[q % 2]
                if q == 0:
                    vtt(d32[:, :, 1:TQ], x32[:, :, 1:TQ],
                        x32[:, :, 0:TQ - 1], ALU.subtract,
                        wait=[rec_free[q % 2]])
                    vcp(d32[:, :, 0:1], x32[:, :, 0:1])
                else:
                    vtt(d32[:], x32[:, :, t0:t0 + TQ],
                        x32[:, :, t0 - 1:t0 + TQ - 1], ALU.subtract,
                        wait=[rec_free[q % 2]])
                for c in range(NCH):
                    lo = 1 if q == 0 else 0
                    vstt(xr[:, c, lo:TQ], d32[:, c, lo:TQ],
                         mur[:, c:c + 1],
                         x32[:, c, t0 + lo - 1:t0 + TQ - 1],
                         ALU.mult, ALU.add)
                    vstt(xk[:, c, lo:TQ], d32[:, c, lo:TQ],
                         muk[:, c:c + 1],
                         x32[:, c, t0 + lo - 1:t0 + TQ - 1],
                         ALU.mult, ALU.add)
                if q == 0:
                    mb = mur[:].rearrange("p (a b) -> p a b", b=1)
                    kb = muk[:].rearrange("p (a b) -> p a b", b=1)
                    vtt(xr[:, :, 0:1], d32[:, :, 0:1], mb, ALU.mult)
                    vtt(xk[:, :, 0:1], d32[:, :, 0:1], kb, ALU.mult)
                prep_done[q] = (s_v, s_v.n)

            prep(0)
            for q in range(T // TQ):
                t0 = q * TQ
                tc = slice(t0, t0 + TQ)
                xr = xr2[q % 2]
                xk = xk2[q % 2]
                kact = xr2[q % 2]
                if q + 1 < T // TQ:
                    prep(q + 1)
                for oc in range(NCH):
                    bk = psF[oc % 2]
                    for c in range(NCH):
                        w0 = ()
                        if c == 0:
                            w0 = [prep_done[q], rec_bank[oc % 2]]
                        mm(bk[:], wf[:, 0, c, oc * 128:(oc + 1) * 128],
                           xr[:, c, :], start=(c == 0), stop=(c == 7),
                           wait=w0, inc=(s_t,) if c == 7 else ())
                    act(rbf[:, oc, :], bk[:], AF.Sigmoid,
                        wait=[(s_t, s_t.n)])
                    rec_bank[oc % 2] = (s_s, s_s.n)
                for oc in range(NCH):
                    bk = psF[oc % 2]
                    for c in range(NCH):
                        w0 = ()
                        if c == 0:
                            w0 = [rec_bank[oc % 2]]
                        mm(bk[:], wf[:, 1, c, oc * 128:(oc + 1) * 128],
                           xk[:, c, :], start=(c == 0), stop=(c == 7),
                           wait=w0, inc=(s_t,) if c == 7 else ())
                    vts(ln_t[:], bk[:], 0.0, ALU.max,
                        wait=[(s_t, s_t.n)])
                    vtt(kact[:, oc, :], ln_t[:], ln_t[:], ALU.mult)
                    rec_bank[oc % 2] = (s_v, s_v.n)
                kact_done = (s_v, s_v.n)
                for oc in range(NCH):
                    bk = psF[oc % 2]
                    for c in range(NCH):
                        w0 = ()
                        if c == 0:
                            w0 = [kact_done] if oc == 0 else []
                            w0 = list(w0) + [rec_bank[oc % 2]]
                        mm(bk[:], wf[:, 2, c, oc * 128:(oc + 1) * 128],
                           kact[:, c, :], start=(c == 0), stop=(c == 7),
                           wait=w0, inc=(s_t,) if c == 7 else ())
                    vtt(tprod[:, 0, :], rbf[:, oc, :], bk[:],
                        ALU.mult, wait=[(s_t, s_t.n)])
                    vtt(x32[:, oc, tc], x32[:, oc, tc],
                        tprod[:, 0, :], ALU.add)
                    rec_bank[oc % 2] = (s_v, s_v.n)
                rec_free[q % 2] = (s_t, s_t.n)
            em.barrier(exclude=em.barrier_exclude)
            emit_ln(sq_scr, outbf=True)

        def emit_l1a():
            ar.reset()
            wout = ar.take((128, 8, 1024), BF16)
            wsl = [ar.take((128, 8, 512), BF16) for _ in range(2)]
            qk_mark = ar.mark()
            qk = ar.take((128, 16, 512), BF16)
            vraw_mark = ar.mark()
            vraw = ar.take((128, 8, 512), BF16)
            rkP = ar.take((128, 8, 512), BF16)
            kPprev = ar.take((128, 8, 128), BF16)
            vtm_mark = ar.mark()
            vtm = ar.take((128, 4, 8, 128), BF16)
            vtmprev = ar.take((128, 8, 128), BF16)
            simsb_mark = ar.mark()
            simsb = ar.take((128, 8, 256), F32)
            pexp2 = [ar.view(simsb_mark, (128, 8, 256), BF16),
                     ar.view(simsb_mark + 4096, (128, 8, 256), BF16)]
            aout_mark = ar.mark()
            aout = ar.take((128, 8, 512), BF16)
            pq_all = ar.view(aout_mark, (128, 8, 512), BF16)
            pk_all = ar.view(simsb_mark, (128, 8, 512), BF16)
            rtmp = ar.view(vtm_mark, (128, 2048), F32)
            pbf = ar.view(vraw_mark, (128, 8, 256), BF16)
            attT = ar.view(vraw_mark + 4096, (128, 8, 256), BF16)
            sq_scr = ar.view(qk_mark, (128, 8, 512), BF16)
            mx = ln_t[:, 0:8]
            nb8 = ln_t[:, 8:16]
            ssum = ln_t[:, 16:24]
            rr = ln_t[:, 24:32]
            GD(lambda e: e.dma_start(
                out=wout, in_=wdr["l1_wout"].rearrange("(c p) o -> p c o",
                                                       p=128)),
                wait=[(s_v, s_v.n), (s_s, s_s.n), (s_t, s_t.n)])
            em.barrier(exclude=em.barrier_exclude)
            t13 = rtmp[:].bitcast(BF16).rearrange(
                "p (a b c) -> p a b c", a=8, b=4, c=128)

            def tbl(name):
                return csb[name][:].rearrange("p (a b) -> p a b", a=1
                                              ).broadcast_to((128, 4, 128))

            SLOT8 = (0, 2, 4, 6, 1, 3, 5, 7)
            rec_avbank = [(s_s, 0)] * 2
            rec_pbf = (s_t, 0)
            exp_rec = [(s_s, 0)] * 4
            pbf_done = {-1: (s_v, 0)}
            sim_done = {}
            attT_done = {}
            av_done = {}

            for ss in range(4):
                c0 = ss * 512
                scs = slice(c0, c0 + 512)
                base_wa, base_wb = s_wa.n, s_wb.n

                def qslab(gg, gate=None):
                    w = [(s_t, s_t.n)]
                    if gate is not None:
                        w = w + [(s_wc, gate)]
                    if ss == 0:
                        GD(lambda e, d=wsl[gg % 2][:, 0:4, :], o0=gg * 512:
                            e.dma_start(out=d, in_=wdr["l1_wqkv"][
                                0:512, o0:o0 + 512].rearrange(
                                "(c p) o -> p c o", p=128)),
                            wait=w, inc=(s_wa,))
                        GD(lambda e, d=wsl[gg % 2][:, 4:8, :], o0=gg * 512:
                            e.dma_start(out=d, in_=wdr["l1_wqkv"][
                                512:1024, o0:o0 + 512].rearrange(
                                "(c p) o -> p c o", p=128)),
                            wait=w, inc=(s_wb,))
                        return
                    SYW2(lambda e, d=wsl[gg % 2][:, 0:4, :], o0=gg * 512:
                        e.dma_start(out=d, in_=wbf["l1_wqkv"][
                            0:512, o0:o0 + 512].rearrange(
                            "(c p) o -> p c o", p=128)), wait=w)
                    SCD(lambda e, d=wsl[gg % 2][:, 4:8, :], o0=gg * 512:
                        e.dma_start(out=d, in_=wbf["l1_wqkv"][
                            512:1024, o0:o0 + 512].rearrange(
                            "(c p) o -> p c o", p=128)), wait=w)
                qslab(0, wc_gate["l1_wqkv"] if ss == 1 else None)
                rec_qbank = [(s_v, 0), (s_s, 0)]
                for g in range(6):
                    if g + 1 < 6:
                        qslab(g + 1)
                    for j in range(4):
                        oc = g * 4 + j
                        bk = psF[oc % 2]
                        for c in range(NCH):
                            w0 = ()
                            if c == 0:
                                w0 = [(s_wa, base_wa + 16 * (g + 1)),
                                      (s_wb, base_wb + 16 * (g + 1)),
                                      rec_qbank[oc % 2]]
                            mm(bk[:], wsl[g % 2][:, c, j * 128:(j + 1) * 128],
                               xbf[:, c, scs], start=(c == 0), stop=(c == 7),
                               wait=w0, inc=(s_t,) if c == 7 else ())
                        dst = (qk[:, oc, :] if oc < 16 else
                               vraw[:, oc - 16, :])
                        if oc % 2 == 0:
                            vcp(dst, bk[:], wait=[(s_t, s_t.n)])
                            rec_qbank[0] = (s_v, s_v.n)
                        else:
                            act(dst, bk[:], AF.Copy, wait=[(s_t, s_t.n)])
                            rec_qbank[1] = (s_s, s_s.n)
                n_qkv_v, n_qkv_s = s_v.n, s_s.n
                for r in range(4):
                    c0r = 2 * r
                    w0 = ([(s_s, s_s.n)] if r > 0 else
                          [(s_v, n_qkv_v), (s_s, n_qkv_s)])
                    mm(psF[2][:], csb["pmat"][:], qk[:, c0r, :],
                       start=True, stop=True, wait=w0)
                    mm(psF[3][:], csb["pmat"][:], qk[:, c0r + 1, :],
                       start=True, stop=True)
                    mm(psF[4][:], csb["pmat"][:], qk[:, 8 + c0r, :],
                       start=True, stop=True)
                    mm(psF[5][:], csb["pmat"][:], qk[:, 9 + c0r, :],
                       start=True, stop=True, inc=(s_t,))
                    act(pq_all[:, c0r:c0r + 2, :], psF6[:, 2:4, :], AF.Copy,
                        wait=[(s_t, s_t.n)], inc=())
                    act(pk_all[:, c0r:c0r + 2, :], psF6[:, 4:6, :], AF.Copy,
                        inc=(s_s,))
                n_rope_s = s_s.n
                q4 = qk[:, 0:8, :].rearrange("p a (b c) -> p a b c", c=128)
                k4 = qk[:, 8:16, :].rearrange("p a (b c) -> p a b c", c=128)
                rp4 = rkP[:].rearrange("p a (b c) -> p a b c", c=128)
                pq4 = pq_all[:].rearrange("p a (b c) -> p a b c", c=128)
                pk4 = pk_all[:].rearrange("p a (b c) -> p a b c", c=128)

                def tbl4(name):
                    return csb[name][:].rearrange(
                        "p (a b c) -> p a b c", a=1, b=1).broadcast_to(
                        (128, 8, 4, 128))
                wt_ = [(s_s, n_rope_s)]
                vtt(t13, pq4, tbl4("ropeQs"), ALU.mult, wait=wt_)
                vtt(q4, q4, tbl4("ropeQc"), ALU.mult)
                vtt(q4, q4, t13, ALU.add)
                vtt(t13, pk4, tbl4("ropePs"), ALU.mult)
                vtt(rp4, k4, tbl4("ropePc"), ALU.mult)
                vtt(rp4, rp4, t13, ALU.add)
                vtt(t13, pk4, tbl4("ropeKs"), ALU.mult)
                vtt(k4, k4, tbl4("ropeKc"), ALU.mult)
                vtt(k4, k4, t13, ALU.add)
                n_chain = s_v.n
                rec_vt = {0: None, 1: None}
                for wl in range(4):
                    w0 = ()
                    if wl == 0:
                        w0 = [(s_v, n_qkv_v), (s_s, n_qkv_s)]
                    elif rec_vt[wl % 2]:
                        w0 = [rec_vt[wl % 2]]
                    for ch in range(NCH):
                        tr(psB[wl % 2][:, ch * 128:(ch + 1) * 128],
                           vraw[:, ch, wl * 128:(wl + 1) * 128],
                           csb["identbf"][:],
                           wait=w0 if ch == 0 else (),
                           inc=(s_t,) if ch == 7 else ())
                    vcp(vtm[:, wl, :, :],
                        psB[wl % 2][:].rearrange("p (a b) -> p a b", a=8),
                        wait=[(s_t, s_t.n)])
                    rec_vt[wl % 2] = (s_v, s_v.n)
                whgs = [(wl, hg) for wl in range(4) for hg in range(2)]
                sim_done.clear()
                pbf_done.clear()
                pbf_done[-1] = pbf_done[-2] = (s_v, 0)
                attT_done.clear()

                def emit_fill_sims(i):
                    wl, hg = whgs[i]
                    w = ss * 4 + wl
                    wc = slice(wl * 128, (wl + 1) * 128)
                    mk = "cmw0" if w == 0 else "cm2"
                    for b in range(4):
                        wf_ = [exp_rec[b]]
                        if i == 0:
                            wf_.append((s_s, n_rope_s))
                        mm(psF[b][:], csb["identbf"][:], csb[mk][:],
                           start=True, stop=True, wait=wf_)
                    for sl in range(8):
                        hh = SLOT8[sl]
                        h = hg * 8 + hh
                        ch = h // 2
                        base = (h % 2) * 64
                        bk = psF[sl // 2]
                        off = (sl % 2) * 256
                        lhs = qk[base:base + 64, ch, wc]
                        if i == 0 and sl == 0:
                            twait([(s_v, n_chain)])
                        if w > 0:
                            rhsP = (kPprev[base:base + 64, ch, :]
                                    if wl == 0 else
                                    rkP[base:base + 64, ch,
                                        (wl - 1) * 128:wl * 128])
                            mm(bk[:, off:off + 128], lhs, rhsP,
                               start=False, stop=True)
                        mm(bk[:, off + 128:off + 256], lhs,
                           qk[base:base + 64, 8 + ch, wc],
                           start=False, stop=True,
                           inc=(s_t,) if sl == 7 else ())
                    sim_done[i] = (s_t, s_t.n)

                def emit_trans(i):
                    nonlocal rec_pbf
                    wl, hg = whgs[i]
                    w = ss * 4 + wl
                    twait([pbf_done[i]])
                    for jc in range(2):
                        if w == 0 and jc == 0:
                            continue
                        for sl in range(8):
                            tr(psB[jc][:, sl * 128:sl * 128 + 128],
                               pbf[:, sl, jc * 128:(jc + 1) * 128],
                               csb["identbf"][:],
                               inc=(s_t,) if sl == 7 else ())
                    rec_pbf = (s_t, s_t.n)

                def emit_exp(i):
                    pe = pexp2[i % 2]
                    act(pe[:],
                        psF6[:, 0:4, :].rearrange("p a (b c) -> p (a b) c",
                                                  c=256),
                        AF.Exp, wait=[sim_done[i], pbf_done[i - 2]],
                        inc=(s_s,))
                    for b in range(4):
                        exp_rec[b] = (s_s, s_s.n)

                def emit_attTcp(i):
                    wl, hg = whgs[i]
                    w = ss * 4 + wl
                    if w == 0:
                        vcp(attT[:, :, 128:256],
                            psB[1][:].rearrange("p (a b) -> p a b", a=8),
                            wait=[rec_pbf])
                    else:
                        vcp(attT[:].rearrange("p a (b c) -> p a b c", c=128),
                            psB2[:].rearrange("p a (b c) -> p b a c", c=128),
                            wait=[rec_pbf])
                    attT_done[i] = (s_v, s_v.n)

                def emit_sumpbf(i):
                    pe = pexp2[i % 2]
                    vred(ssum[:, 0:8], pe[:], ALU.add,
                         wait=[(s_s, exp_rec[3][1])])
                    vrecip(rr[:, 0:8], ssum[:, 0:8])
                    vtt(pbf[:], pe[:],
                        rr[:, 0:8].rearrange("p (a b) -> p a b", b=1
                                             ).broadcast_to((128, 8, 256)),
                        ALU.mult, wait=[rec_pbf])
                    pbf_done[i] = (s_v, s_v.n)

                def emit_avm(i):
                    wl, hg = whgs[i]
                    w = ss * 4 + wl
                    for sl in range(8):
                        hh = SLOT8[sl]
                        h = hg * 8 + hh
                        ch = h // 2
                        bk = psF[4 + sl // 4]
                        off = (sl % 4) * 128
                        w0 = []
                        if sl == 0:
                            w0 = [attT_done[i], rec_avbank[0]]
                        elif sl == 4:
                            w0 = [rec_avbank[1]]
                        if w > 0:
                            vprev = (vtmprev[:, ch, :] if wl == 0 else
                                     vtm[:, wl - 1, ch, :])
                            mm(bk[:, off:off + 128], vprev,
                               attT[:, sl, 0:128], start=True,
                               stop=False, wait=w0)
                            w0 = []
                        mm(bk[:, off:off + 128], vtm[:, wl, ch, :],
                           attT[:, sl, 128:256], start=(w == 0),
                           stop=True, wait=w0,
                           inc=(s_t,) if sl == 7 else ())
                    av_done[i] = (s_t, s_t.n)

                def emit_ave(i):
                    wl, hg = whgs[i]
                    wc = slice(wl * 128, (wl + 1) * 128)
                    for half in range(2):
                        b0 = half * 64
                        act(aout[b0:b0 + 64, hg * 4:hg * 4 + 4, wc],
                            psF[4 + half][b0:b0 + 64, :].rearrange(
                                "p (a b) -> p a b", a=4), AF.Copy,
                            wait=[av_done[i]] if half == 0 else (),
                            inc=(s_s,))
                        rec_avbank[half] = (s_s, s_s.n)

                for i in range(9):
                    if i < 8:
                        emit_fill_sims(i)
                    if i >= 1:
                        emit_trans(i - 1)
                    if i < 8:
                        emit_exp(i)
                    if i >= 1:
                        emit_attTcp(i - 1)
                    if i < 8:
                        emit_sumpbf(i)
                    if i >= 1:
                        emit_avm(i - 1)
                        emit_ave(i - 1)
                vcp(kPprev[:], rkP[:, :, 384:512], wait=[(s_t, s_t.n)])
                vcp(vtmprev[:], vtm[:, 3, :, :])
                rec_op = {0: None, 1: None}
                for oc in range(NCH):
                    bk = psF[oc % 2]
                    for c in range(NCH):
                        w0 = ()
                        if c == 0:
                            w0 = ([(s_v, s_v.n), (s_s, s_s.n)]
                                  if oc == 0 else
                                  ([rec_op[oc % 2]] if rec_op[oc % 2]
                                   else []))
                        mm(bk[:], wout[:, c, oc * 128:(oc + 1) * 128],
                           aout[:, c, :], start=(c == 0), stop=(c == 7),
                           wait=w0, inc=(s_t,) if c == 7 else ())
                    vtt(x32[:, oc, scs], x32[:, oc, scs], bk[:], ALU.add,
                        wait=[(s_t, s_t.n)])
                    rec_op[oc % 2] = (s_v, s_v.n)
                if ss == 0:
                    emit_cache([k for k in ("l1_h2", "l1_w3")
                                if k not in wc_gate])
                em.barrier(exclude=em.barrier_exclude)
            emit_ln(sq_scr, outbf=True)

        def emit_l1f():
            ar.reset()
            wsl_mark = [ar.mark()]
            wsl = []
            for _ in range(2):
                wsl_mark.append(ar.mark())
                wsl.append(ar.take((128, 32, 256), BF16))
            wslA = [ar.view(wsl_mark[1], (128, 16, 512), BF16),
                    ar.view(wsl_mark[2], (128, 16, 512), BF16)]
            h_mark = ar.mark()
            h_sb = ar.take((128, 32, 512), BF16)
            g_mark = ar.mark()
            g_sb = ar.take((128, 32, 512), BF16)
            s1, s2 = smalls["l1_s1"], smalls["l1_s2"]
            sh1, sh2, s3 = (smalls["l1_sh1"], smalls["l1_sh2"],
                            smalls["l1_s3"])
            silt = ar.view(g_mark, (128, 2, 512), F32)
            sq_scr = ar.view(h_mark, (128, 8, 512), BF16)
            otm = [ar.view(g_mark + 4096 + 2048 * k, (128, 512), F32)
                   for k in range(4)]
            emit_cache([k for k in ("l1_w1", "l1_w2", "l1_h1", "l1_h2",
                                    "l1_w3") if k not in wc_gate])
            em.barrier(exclude=em.barrier_exclude)

            nD_v = s_v.n
            pend = None
            pending_store = None

            def emit_store(st):
                for tb in range(4):
                    trow = st * 512 + tb * 128
                    for cg in range(2):
                        bank = psF[4 + cg]
                        ob = otm[(tb % 2) * 2 + cg]
                        for j in range(4):
                            tr(bank[:, j * 128:(j + 1) * 128],
                               x32[:, cg * 4 + j, trow:trow + 128],
                               csb["ident32"][:],
                               wait=[(s_st, max(0, s_st.n - 32)),
                                     (s_v, s_v.n)] if j == 0 else (),
                               inc=(s_t,) if j == 3 else ())
                        vcp(ob, bank[:], wait=[(s_t, s_t.n)])
                        SY(lambda e, o=xout[trow:trow + 128,
                                            cg * 512:cg * 512 + 512],
                           i=ob: e.dma_start(out=o, in_=i),
                           wait=[(s_v, s_v.n)], inc=(s_st,))
                ln_rec["m"] = (s_v, s_v.n)
                ln_rec["v"] = (s_v, s_v.n)

            for tt in range(4):
                tc = slice(tt * 512, tt * 512 + 512)

                def slabA(gg, gate=None):
                    w = [(s_t, s_t.n)]
                    if gate is not None:
                        w = w + [(s_wc, gate)]
                    SYW2(lambda e, d=wslA[gg % 2][:, 0:8, :], o0=gg * 512:
                        e.dma_start(
                        out=d, in_=wbf["l1_w1"][:, o0:o0 + 512].rearrange(
                            "(c p) o -> p c o", p=128)), wait=w)
                    SCD(lambda e, d=wslA[gg % 2][:, 8:16, :], o0=gg * 512:
                        e.dma_start(
                        out=d, in_=wbf["l1_w2"][:, o0:o0 + 512].rearrange(
                            "(c p) o -> p c o", p=128)), wait=w)

                def slabH(gg, wn, gate=None):
                    w = [(s_t, s_t.n)]
                    if gate is not None:
                        w = w + [(s_wc, gate)]
                    SYW2(lambda e, d=wsl[gg % 2][:, 0:16, :], o0=gg * 256:
                        e.dma_start(out=d, in_=wbf[wn][
                            0:2048, o0:o0 + 256].rearrange(
                            "(c p) o -> p c o", p=128)), wait=w)
                    SCD(lambda e, d=wsl[gg % 2][:, 16:32, :], o0=gg * 256:
                        e.dma_start(out=d, in_=wbf[wn][
                            2048:4096, o0:o0 + 256].rearrange(
                            "(c p) o -> p c o", p=128)), wait=w)

                def slabD(gg, gate=None):
                    slabH(gg, "l1_w3", gate)

                gt = (lambda k: wc_gate[k] if tt == 0 else None)
                stage_w = [(s_v, nD_v)]
                if pend is None:
                    base_wa, base_wb = s_wa.n, s_wb.n
                    slabA(0, (max(wc_gate["l1_w1"], wc_gate["l1_w2"])
                              if tt == 0 else None))
                else:
                    base_wa, base_wb = pend
                    pend = None
                rec_silu = {0: (s_s, 0), 1: (s_s, 0)}
                rec_stt = {0: (s_v, 0), 1: (s_v, 0)}
                for g in range(8):
                    if g == 1 and pending_store is not None:
                        emit_store(pending_store)
                        pending_store = None
                    if g + 1 < 8:
                        slabA(g + 1)
                    else:
                        pend = (s_wa.n, s_wb.n)
                        slabH(0, "l1_h1", gt("l1_h1"))
                    w1v = wslA[g % 2]
                    for j in range(4):
                        oc = g * 4 + j
                        b = j % 2
                        bk1 = psF[b]
                        bk2 = psF[2 + b]
                        for c in range(NCH):
                            w0 = ()
                            if c == 0:
                                w0 = [(s_wa, base_wa + 16 * (g + 1)),
                                      (s_wb, base_wb + 16 * (g + 1)),
                                      rec_silu[b]]
                                if g == 0 and j == 0:
                                    w0 += stage_w
                            mm(bk1[:], w1v[:, c, j * 128:(j + 1) * 128],
                               xbf[:, c, tc], start=(c == 0), stop=(c == 7),
                               wait=w0)
                        for c in range(NCH):
                            mm(bk2[:], w1v[:, 8 + c, j * 128:(j + 1) * 128],
                               xbf[:, c, tc], start=(c == 0), stop=(c == 7),
                               wait=[rec_stt[b]] if c == 0 else (),
                               inc=(s_t,) if c == 7 else ())
                        act(silt[:, b, :], bk1[:], AF.Silu,
                            scale=s1[:, oc:oc + 1],
                            wait=[(s_t, s_t.n), rec_stt[b]], inc=(s_s,))
                        rec_silu[b] = (s_s, s_s.n)
                        vstt(h_sb[:, oc, :], bk2[:], s2[:, oc:oc + 1],
                             silt[:, b, :], ALU.mult, ALU.mult,
                             wait=[rec_silu[b]])
                        rec_stt[b] = (s_v, s_v.n)
                stage_w = [(s_v, s_v.n), (s_s, s_s.n), (s_st, s_st.n)]
                for wname, ssc, src_sb, dst_sb in (
                        ("l1_h1", sh1, h_sb, g_sb), ("l1_h2", sh2, g_sb,
                                                     h_sb)):
                    if pend is None:
                        base_wa, base_wb = s_wa.n, s_wb.n
                        slabH(0, wname, gt(wname))
                    else:
                        base_wa, base_wb = pend
                        pend = None
                    rec_gelu = {0: (s_s, 0), 1: (s_s, 0)}
                    for g in range(16):
                        if g + 1 < 16:
                            slabH(g + 1, wname)
                        else:
                            pend = (s_wa.n, s_wb.n)
                            if wname == "l1_h1":
                                slabH(0, "l1_h2", gt("l1_h2"))
                            else:
                                slabD(0, gt("l1_w3"))
                        for j in range(2):
                            oc = g * 2 + j
                            bk = psF[j]
                            for c in range(32):
                                w0 = ()
                                if c == 0:
                                    w0 = [(s_wa, base_wa + 16 * (g + 1)),
                                          (s_wb, base_wb + 16 * (g + 1)),
                                          rec_gelu[j]]
                                    if g == 0 and j == 0:
                                        w0 += stage_w
                                mm(bk[:], wsl[g % 2][:, c,
                                                     j * 128:(j + 1) * 128],
                                   src_sb[:, c, :], start=(c == 0),
                                   stop=(c == 31), wait=w0,
                                   inc=(s_t,) if c == 31 else ())
                            act(dst_sb[:, oc, :], bk[:], AF.Gelu,
                                scale=ssc[:, oc:oc + 1], wait=[(s_t, s_t.n)])
                            rec_gelu[j] = (s_s, s_s.n)
                    stage_w = [(s_s, s_s.n)]
                if pend is None:
                    base_wa, base_wb = s_wa.n, s_wb.n
                    slabD(0, gt("l1_w3"))
                else:
                    base_wa, base_wb = pend
                    pend = None
                rec_d = {0: (s_v, 0), 1: (s_v, 0)}
                for g in range(4):
                    if g + 1 < 4:
                        slabD(g + 1)
                    elif tt + 1 < 4:
                        pend = (s_wa.n, s_wb.n)
                        slabA(0)
                    for j in range(2):
                        oc = g * 2 + j
                        bk = psF[j]
                        for c in range(32):
                            w0 = ()
                            if c == 0:
                                w0 = [(s_wa, base_wa + 16 * (g + 1)),
                                      (s_wb, base_wb + 16 * (g + 1)),
                                      rec_d[j]]
                                if g == 0 and j == 0:
                                    w0 += stage_w
                            mm(bk[:], wsl[g % 2][:, c, j * 128:(j + 1) * 128],
                               h_sb[:, c, :], start=(c == 0), stop=(c == 31),
                               wait=w0, inc=(s_t,) if c == 31 else ())
                        vstt(x32[:, oc, tc], bk[:], s3[:, oc:oc + 1],
                             x32[:, oc, tc], ALU.mult, ALU.add,
                             wait=[(s_t, s_t.n)])
                        rec_d[j] = (s_v, s_v.n)
                nD_v = s_v.n
                emit_ln_tile(tt, sq_scr, outbf=False)
                if out_tok:
                    if tt + 1 < 4:
                        pending_store = tt
                    else:
                        emit_store(tt)
            em.barrier(exclude=em.barrier_exclude)

        for ph in phases:
            dict(l0a=emit_l0a, l0f=emit_l0f, l1a=emit_l1a,
                 l1f=emit_l1f)[ph]()

        if not out_tok:
            em.barrier(exclude=em.barrier_exclude)
            SY(lambda e: e.dma_start(
                out=xout.rearrange("(c p) t -> p c t", p=128), in_=x32[:]),
                inc=(s_st,))
        em.op("sync", None, [(s_st, s_st.n)], [])

        for s in em.sems:
            s.h = E(nc.semaphore(s.name))
        block = E(nc.Block())

        def make_body(items):
            def body(e):
                for _, fn, waits, incs in items:
                    for (sm, val) in waits:
                        if val > 0:
                            e.wait_ge(sm.h, val)
                    ins = fn(e) if fn is not None else None
                    if incs:
                        if ins is None:
                            ins = e.nop()
                        for (sm, k) in incs:
                            ins.then_inc(sm.h, k)
            return body

        for eng in ENGINES:
            items = [it for it in em.items if it[0] == eng]
            if items:
                getattr(block, eng)(make_body(items))

    return nc, names


_PROG_CACHE = {}


def _get_prog(phases):
    key = tuple(phases)
    if key not in _PROG_CACHE:
        _PROG_CACHE[key] = build_program(list(phases))
    return _PROG_CACHE[key]


LAUNCH_GROUPS = [("l0a", "l0f", "l1a", "l1f")]


def kernel(**inputs):
    w = host_weights(inputs)
    c = host_consts()
    x = _f32(inputs["L0_attn_x"])
    cur = [np.ascontiguousarray(x[b]) for b in range(NC)]

    for phases in LAUNCH_GROUPS:
        nc, names = _get_prog(phases)
        in_maps = []
        for b in range(NC):
            m = {}
            for nm in names:
                if nm in ("xin", "xin_fm"):
                    m[nm] = cur[b]
                elif nm in c:
                    m[nm] = c[nm]
                else:
                    m[nm] = w[nm]
            in_maps.append(m)
        res = run_bass_kernel_spmd(nc, in_maps, core_ids=list(range(NC)))
        okey = "xout" if phases[-1] == "l1f" else "xout_fm"
        cur = [np.ascontiguousarray(np.asarray(res.results[b][okey]))
               for b in range(NC)]

    return np.stack(cur, axis=0).astype(np.float32)
